# version 19
# baseline (speedup 1.0000x reference)
import numpy as np
import ml_dtypes
import concourse.bass as bass
import concourse.mybir as mybir
from concourse.bass_utils import run_bass_kernel_spmd

F32 = mybir.dt.float32
BF16 = mybir.dt.bfloat16
I8 = mybir.dt.int8
AF = mybir.ActivationFunctionType
ALU = mybir.AluOpType
AXX = mybir.AxisListType.X

T = 512
NPASS = 4
HALO = 8
TT = T + HALO
H = 16
KC = 16
MC = 64
OC = 16
HID = 2048
INTER = 8192
NCORE = 8

CF_SGATE = 0
CF_SUP = 64
CF_SDOWN = 128
CF_NORMW = 144
CF_CONVQ = 160
CF_CONVK = 224
CF_CONVV = 288
CF_COS = 352
CF_SINP = 355
CF_ONE = 358
CF_MEANBC = 360
CF_ONEBC = 488
CF_ONE16 = 616
CF_SDH = 617
CF_LNEPS = 633
CF_RMSEPS = 634
CF_INVD = 635
NCF = 640

CB_HO1 = 0
CB_HOS = 256
CB_ONECOL = 512
CB_EH = 516
NCB = 516 + 2048

WS_GU = 0
WS_DN = 8192
WS_QKV = 0
WS_Z = 24576
WS_WOUT = 0
WS_WAB = 28672
NWS = 29184

AR_FUSE = 0
AR_FUSEH = 32768
AR_CV = 0
AR_OUT = 8320
AR_SQ = 16512
AR_SK = 17032
AR_SV = 17552
AR_CQ = 18072
AR_CK = 18592
AR_KSW = 19112
AR_RK = 19632
AR_QQ = 20152
AR_KK = 20664
AR_DM = 21176
AR_WBC0 = 21688
AR_WBC1 = 22200
AR_LT0 = 22712
AR_LT1 = 23232
AR_W4B = 23752
AR_G4B = 25800
AR_SQ2 = 27848
AR_DM1 = 28896
AR_DM2 = 29408
AR_DM3 = 29920
AR_DMH = 30432
AR_SQB = 30440
AR_SKB = 30960
AR_SVB = 31480
AR_CKH = 32000
AR_QQB = 32080
AR_KKB = 32592
NAR = 33280

RW_K2 = 0
RW_A = 520
RW_S1 = 1040
RW_RQ = 1560
RW_RK = 2072
RW_S3 = 2592
NRW = 3112


class Em:
    ENGS = ("sync", "gpsimd", "tensor", "vector", "scalar")

    def __init__(self):
        self.ops = {e: [] for e in self.ENGS}
        self.cnt = {}
        self.marks = []

    def mark(self, name):
        self.marks.append((name, len(self.ops["tensor"])))

    def sem(self, name):
        self.cnt[name] = 0

    def emit(self, eng, fn, wait=(), inc=()):
        wait = tuple(t for t in wait if t is not None)
        self.ops[eng].append((fn, wait, tuple(inc)))
        toks = []
        for s, n in inc:
            self.cnt[s] += n
            toks.append((s, self.cnt[s]))
        return toks[0] if len(toks) == 1 else (toks if toks else None)


def later(*toks):
    out = []
    for t in toks:
        if t is None:
            continue
        if isinstance(t, list):
            out.extend(x for x in t if x is not None)
        else:
            out.append(t)
    return out


def build_program():
    nc = bass.Bass("TRN2", target_bir_lowering=False)

    xT = nc.dram_tensor("xT", [HID, NPASS * T], F32, kind="ExternalInput").ap()
    xH = nc.dram_tensor("xH", [128, KC * HALO], F32, kind="ExternalInput").ap()
    wgu = nc.dram_tensor("wgu", [MC, 128, 2 * KC * 128], I8, kind="ExternalInput").ap()
    wdn = nc.dram_tensor("wdn", [OC, 128, MC * 128], I8, kind="ExternalInput").ap()
    wqkv = nc.dram_tensor("wqkv", [OC, 128, 3 * KC * 128], F32, kind="ExternalInput").ap()
    wz = nc.dram_tensor("wz", [OC, 128, KC * 128], F32, kind="ExternalInput").ap()
    wou = nc.dram_tensor("wou", [OC, 128, KC * 128], F32, kind="ExternalInput").ap()
    wab = nc.dram_tensor("wab", [128, KC * 32], F32, kind="ExternalInput").ap()
    cfd = nc.dram_tensor("cfd", [128, NCF], F32, kind="ExternalInput").ap()
    cbd = nc.dram_tensor("cbd", [128, NCB], BF16, kind="ExternalInput").ap()
    yT = nc.dram_tensor("yT", [HID, NPASS * T], F32, kind="ExternalOutput").ap()
    xTr = xT.rearrange("(c p) t -> c p t", p=128)
    yTr = yT.rearrange("(c p) t -> c p t", p=128)

    em = Em()
    for s in ("LX", "ST", "SW", "W0", "W1", "W2", "W3", "WB", "PE", "DV", "AC", "GP"):
        em.sem(s)

    from contextlib import ExitStack
    with ExitStack() as _st:
        def _e(cm):
            return _st.enter_context(cm)
        xt = _e(nc.sbuf_tensor("xt", [128, KC * T], F32))
        xh = _e(nc.sbuf_tensor("xh", [128, KC * HALO], F32))
        xln = _e(nc.sbuf_tensor("xln", [128, KC * T], BF16))
        xlnh = _e(nc.sbuf_tensor("xlnh", [128, KC * HALO], BF16))
        siluz = _e(nc.sbuf_tensor("siluz", [128, KC * T], BF16))
        ar = _e(nc.sbuf_tensor("arena", [128, NAR], BF16))
        ws = _e(nc.sbuf_tensor("wst", [128, NWS], BF16))
        cf = _e(nc.sbuf_tensor("cf", [128, NCF], F32))
        cb = _e(nc.sbuf_tensor("cb", [128, NCB], BF16))
        rows = _e(nc.sbuf_tensor("rows", [128, NRW], F32))
        carSq = _e(nc.sbuf_tensor("carSq", [128, H * HALO], BF16))
        carSk = _e(nc.sbuf_tensor("carSk", [128, H * HALO], BF16))
        carSv = _e(nc.sbuf_tensor("carSv", [128, H * HALO], BF16))
        carCk = _e(nc.sbuf_tensor("carCk", [128, H * HALO], BF16))
        carCv = _e(nc.sbuf_tensor("carCv", [128, H * HALO], BF16))
        P0 = _e(nc.psum_tensor("P0", [128, T], F32))
        P1 = _e(nc.psum_tensor("P1", [128, T], F32))
        P2 = _e(nc.psum_tensor("P2", [128, T], F32))
        P3 = _e(nc.psum_tensor("P3", [128, T], F32))
        P4 = _e(nc.psum_tensor("P4", [128, T], F32))
        P5 = _e(nc.psum_tensor("P5", [128, T], F32))
        P6 = _e(nc.psum_tensor("P6", [128, T], F32))
        P7 = _e(nc.psum_tensor("P7", [128, T], F32))
        sLX = _e(nc.semaphore("LX"))
        sST = _e(nc.semaphore("ST"))
        sSW = _e(nc.semaphore("SW"))
        sW0 = _e(nc.semaphore("W0"))
        sW1 = _e(nc.semaphore("W1"))
        sPE = _e(nc.semaphore("PE"))
        sDV = _e(nc.semaphore("DV"))
        sAC = _e(nc.semaphore("AC"))
        sWB = _e(nc.semaphore("WB"))
        sGP = _e(nc.semaphore("GP"))
        sW2 = _e(nc.semaphore("W2"))
        sW3 = _e(nc.semaphore("W3"))
        block = _e(nc.Block())
        sems = {"LX": sLX, "ST": sST, "SW": sSW, "W0": sW0, "W1": sW1,
                "WB": sWB, "PE": sPE, "DV": sDV, "AC": sAC, "GP": sGP,
                "W2": sW2, "W3": sW3}
        P = [P0, P1, P2, P3, P4, P5, P6, P7]

        bankfree = [[] for _ in range(8)]

        def bf(i, cond=True):
            return list(bankfree[i]) if cond else []

        def setbf(i, *toks):
            bankfree[i] = [t for t in toks if t is not None]
        slot_use = {"gu": None, "dn": None, "qkv": None, "z": None, "wout": None}
        w_chain = {"W0": None, "W1": None, "W2": None, "W3": None}
        tile_free = {"qq": None, "kk": None, "sq0": None, "sk0": None,
                     "sv0": None, "sq1": None, "sk1": None, "sv1": None,
                     "k2row": None, "bcs": None, "ckrd": None, "late_kk": None}
        for _p in range(2):
            for _s in range(4):
                tile_free["dm%d%d" % (_s, _p)] = None
            tile_free["qq%d" % _p] = None
            tile_free["kk%d" % _p] = None
        arena_tok = [None]
        convv_free = [None] * H

        def _ap(x):
            if isinstance(x, bass.AP) or x is None or isinstance(x, (int, float)):
                return x
            try:
                return x[:, :]
            except Exception:
                return x

        def dma(q, out, in_, sem, wait=()):
            return em.emit(q, lambda e, o=out, i=in_: e.dma_start(out=o, in_=i),
                           wait=wait, inc=((sem, 16),))

        def mm(out, lhsT, rhs, start, stop, wait=(), inc=()):
            return em.emit("tensor",
                           lambda e, o=_ap(out), l=_ap(lhsT), r=_ap(rhs), st=start, sp=stop:
                           e.matmul(o, l, r, start=st, stop=sp),
                           wait=wait, inc=inc)

        def act(out, in_, func, wait=(), scale=None, bias=None, inc=(("AC", 1),)):
            wait = list(wait) + [("AC", em.cnt["AC"])]
            kw = {}
            if scale is not None:
                kw["scale"] = scale
            if bias is not None:
                kw["bias"] = bias
            return em.emit("scalar",
                           lambda e, o=_ap(out), i=_ap(in_), f=func, kw=kw: e.activation(o, i, f, **kw),
                           wait=wait, inc=inc)

        def gop(name, *args, wait=(), inc=(("GP", 1),), **kwargs):
            wait = list(wait) + [("GP", em.cnt["GP"])]
            args = tuple(_ap(a) for a in args)
            def fn(e, n=name, a=args, kw=kwargs):
                return getattr(e, n)(*a, **kw)
            return em.emit("gpsimd", fn, wait=wait, inc=inc)

        def vop(name, *args, wait=(), inc=(("DV", 1),), **kwargs):
            wait = list(wait) + [("DV", em.cnt["DV"])]
            args = tuple(_ap(a) for a in args)
            def fn(e, n=name, a=args, kw=kwargs):
                return getattr(e, n)(*a, **kw)
            return em.emit("vector", fn, wait=wait, inc=inc)

        def xt_c(c, lo=0, hi=T):
            return xt[:, c * T + lo: c * T + hi]

        def xln_c(c, lo=0, hi=T):
            return xln[:, c * T + lo: c * T + hi]

        def xh_c(c):
            return xh[:, c * HALO:(c + 1) * HALO]

        def xlnh_c(c):
            return xlnh[:, c * HALO:(c + 1) * HALO]

        def arr(base, lo, hi, parts=(0, 128)):
            return ar[parts[0]:parts[1], base + lo: base + hi]

        def rws(base, lo, hi, parts=(0, 16)):
            return rows[parts[0]:parts[1], base + lo: base + hi]

        dma("sync", cf[:, :], cfd[:, :], "LX")
        dma("sync", cb[:, :], cbd[:, :], "LX")
        tok3 = dma("sync", xh[:, :], xH[:, :], "LX")
        em.emit("sync", lambda e: e.nop(), wait=[tok3])
        tok_cf = tok_cb = tok_xh = tok3
        tok_wab = dma("gpsimd", ws[:, WS_WAB:WS_WAB + 512], wab[:, :], "WB", wait=())
        tok_sdh = vop("tensor_scalar", cf[:, CF_SDH:CF_SDH + 16],
                      cf[:, CF_SDOWN:CF_SDOWN + 16], 0.5, None,
                      wait=[tok_cf], op0=ALU.mult)
        tok_z1 = vop("memset", rows[0:16, RW_K2:RW_K2 + 8], 0.0)
        tok_z2 = vop("memset", rows[0:16, RW_A:RW_A + 8], 0.0)

        def ln_stats(src_c, single=False):
            if single:
                sqpar = [ar[:, AR_FUSEH:AR_FUSEH + T]] * 2
            else:
                sqpar = [ar[:, AR_SQ2 + i * T: AR_SQ2 + (i + 1) * T] for i in range(2)]
            st = {"sx": None, "sx2": None, "sqmm": [None, None]}

            def cbk(c, tok):
                par = c % 2
                a_t = act(sqpar[par], src_c(c), AF.Square,
                          wait=later(tok,
                                     st["sx2"] if single else st["sqmm"][par],
                                     arena_tok[0] if (not single and c == 0) else None))
                st["sx2"] = mm(P7[0:1, :], cb[:, CB_ONECOL:CB_ONECOL + 1], sqpar[par],
                               c == 0, c == KC - 1,
                               wait=later(a_t, *bf(7, c == 0)),
                               inc=(("PE", 1),))
                st["sqmm"][par] = st["sx2"]
                st["sx"] = mm(P6[0:1, :], cf[:, CF_ONE:CF_ONE + 1], src_c(c),
                              c == 0, c == KC - 1,
                              wait=later(tok, *bf(6, c == 0)),
                              inc=((("PE", 1),) if c == KC - 1 else ()))
            st["cb"] = cbk
            return st

        def emit_LN(src_c, src_h, out_ln, out_lnh, src_tok_c, halo, tag, stats=None):
            sqh = [ar[:, AR_SQ2 + 2 * T + i * HALO: AR_SQ2 + 2 * T + (i + 1) * HALO]
                   for i in range(2)]
            inphase = stats is None
            if inphase:
                stats = ln_stats(src_c)
                for c in range(KC):
                    stats["cb"](c, src_tok_c(c))
            tok_sx, tok_sx2 = stats["sx"], stats["sx2"]
            tok_stats_h = None
            if halo:
                sqh_tok = [None, None]
                for c in range(KC):
                    par = c % 2
                    a_t = act(sqh[par], src_h(c), AF.Square,
                              wait=later(src_tok_c(c), sqh_tok[par]))
                    sqh_tok[par] = mm(P2[0:1, 8:16], cb[:, CB_ONECOL:CB_ONECOL + 1],
                                      sqh[par], c == 0, c == KC - 1,
                                      wait=later(a_t, *bf(2, c == 0)),
                                      inc=(("PE", 1),))
                for c in range(KC):
                    tok_stats_h = mm(P2[0:1, 0:8], cf[:, CF_ONE:CF_ONE + 1], src_h(c),
                                     c == 0, c == KC - 1,
                                     inc=((("PE", 1),) if c == KC - 1 else ()))
            w = later(tok_sx, tok_sx2, tok_stats_h)
            tA = vop("tensor_copy", rows[0:1, RW_S1 + 8:RW_S1 + 8 + T], P6[0:1, :], wait=w)
            tB = vop("tensor_copy", rows[0:1, RW_S3 + 8:RW_S3 + 8 + T], P7[0:1, :], wait=w)
            if halo:
                tA = vop("tensor_copy", rows[0:1, RW_S1:RW_S1 + 8], P2[0:1, 0:8], wait=w)
                tB = vop("tensor_copy", rows[0:1, RW_S3:RW_S3 + 8], P2[0:1, 8:16], wait=w)
            setbf(6, tA)
            setbf(7, tB)
            setbf(2, tB)
            lo, hi = (0, TT) if halo else (8, TT)
            bcm = mm(P6, cf[0:1, CF_MEANBC:CF_MEANBC + 128],
                     rows[0:1, RW_S1 + 8:RW_S1 + 8 + T], True, True,
                     wait=later(tA, *bf(6)), inc=(("PE", 1),))
            bcmh = None
            if halo:
                bcmh = mm(P5[:, 0:8], cf[0:1, CF_MEANBC:CF_MEANBC + 128],
                          rows[0:1, RW_S1:RW_S1 + 8], True, True,
                          wait=later(tA, *bf(5)), inc=(("PE", 1),))
            t1 = vop("tensor_mul", rows[0:1, RW_S1 + lo:RW_S1 + hi],
                     rows[0:1, RW_S1 + lo:RW_S1 + hi], rows[0:1, RW_S1 + lo:RW_S1 + hi],
                     wait=later(tA, bcm, bcmh))
            tv = vop("scalar_tensor_tensor", rows[0:1, RW_S3 + lo:RW_S3 + hi],
                     rows[0:1, RW_S1 + lo:RW_S1 + hi], -1.0 / HID,
                     rows[0:1, RW_S3 + lo:RW_S3 + hi],
                     wait=[t1, tB], op0=ALU.mult, op1=ALU.add)
            tsq = act(rows[0:1, RW_S1 + lo:RW_S1 + hi], rows[0:1, RW_S3 + lo:RW_S3 + hi],
                      AF.Sqrt, scale=cf[0:1, CF_INVD:CF_INVD + 1],
                      bias=cf[0:1, CF_LNEPS:CF_LNEPS + 1], wait=[tv])
            tr = vop("reciprocal", rows[0:1, RW_S3 + lo:RW_S3 + hi],
                     rows[0:1, RW_S1 + lo:RW_S1 + hi], wait=[tsq])
            bcr = mm(P7, cf[0:1, CF_ONEBC:CF_ONEBC + 128],
                     rows[0:1, RW_S3 + 8:RW_S3 + 8 + T], True, True,
                     wait=later(tr, *bf(7)), inc=(("PE", 1),))
            bcrh = None
            if halo:
                bcrh = mm(P5[:, 8:16], cf[0:1, CF_ONEBC:CF_ONEBC + 128],
                          rows[0:1, RW_S3:RW_S3 + 8], True, True,
                          wait=[tr], inc=(("PE", 1),))
            toks = []
            for c in range(KC):
                t_s = vop("tensor_sub", out_ln(c), src_c(c), P6,
                          wait=later(bcm, bcr))
                toks.append(vop("tensor_mul", out_ln(c), out_ln(c), P7, wait=[t_s]))
            tok_h = None
            if halo:
                for c in range(KC):
                    t_s = vop("tensor_sub", out_lnh(c), src_h(c), P5[:, 0:8],
                              wait=later(bcmh, bcrh))
                    tok_h = vop("tensor_mul", out_lnh(c), out_lnh(c), P5[:, 8:16],
                                wait=[t_s])
                setbf(5, tok_h)
            setbf(6, toks[-1])
            setbf(7, toks[-1])
            if inphase:
                tile_free["dm31"] = tok_sx2
            return toks, tok_h

        def emit_FFN(ln_toks, ln_tok_h, halo, next_stats=None):
            gu_used = [None] * MC
            fuse_tok = None
            first_ffn_write = [later(arena_tok[0])]
            for m in range(MC):
                par = m % 2
                wsem = "W0" if par == 0 else "W1"
                wwait = later(w_chain[wsem],
                              slot_use["qkv"] if m < 2 else None,
                              slot_use["wout"] if m < 2 else None)
                wtok = dma("gpsimd", ws[:, WS_GU + par * 4096: WS_GU + (par + 1) * 4096],
                           wgu[m, :, :], wsem, wait=wwait)
                gslc = lambda k, pp=par: ws[:, WS_GU + pp * 4096 + k * 128: WS_GU + pp * 4096 + (k + 1) * 128]
                uslc = lambda k, pp=par: ws[:, WS_GU + pp * 4096 + 2048 + k * 128: WS_GU + pp * 4096 + 2048 + (k + 1) * 128]
                tG = tU = None
                for k in range(KC):
                    tG = mm(P[par], gslc(k), xln_c(k), k == 0, k == KC - 1,
                            wait=later(wtok, ln_toks[k], *bf(par, k == 0)),
                            inc=((("PE", 1),) if k == KC - 1 else ()))
                upar = 2 if par == 0 else 5
                for k in range(KC):
                    tU = mm(P[upar], uslc(k), xln_c(k), k == 0, k == KC - 1,
                            wait=later(*bf(upar, k == 0)),
                            inc=((("PE", 1),) if k == KC - 1 else ()))
                tGh = tUh = None
                if halo:
                    hpar = 6 if par == 0 else 7
                    for k in range(KC):
                        tGh = mm(P[hpar][:, 0:8], gslc(k), xlnh_c(k), k == 0, k == KC - 1,
                                 wait=later(ln_tok_h if k == 0 else None, *bf(hpar, k == 0)),
                                 inc=((("PE", 1),) if k == KC - 1 else ()))
                    for k in range(KC):
                        tUh = mm(P[hpar][:, 8:16], uslc(k), xlnh_c(k), k == 0, k == KC - 1,
                                 inc=((("PE", 1),) if k == KC - 1 else ()))
                gu_used[m] = tUh if halo else tU
                w_chain[wsem] = gu_used[m]
                tg = act(ar[:, AR_FUSE + m * T: AR_FUSE + (m + 1) * T], P[par], AF.Silu,
                         scale=cf[:, CF_SGATE + m:CF_SGATE + m + 1],
                         wait=later(tG, *first_ffn_write[0]))
                first_ffn_write[0] = []
                setbf(par, tg)
                tf = vop("scalar_tensor_tensor", ar[:, AR_FUSE + m * T: AR_FUSE + (m + 1) * T],
                         P[upar], cf[:, CF_SUP + m:CF_SUP + m + 1],
                         ar[:, AR_FUSE + m * T: AR_FUSE + (m + 1) * T],
                         wait=[tU, tg], op0=ALU.mult, op1=ALU.mult)
                setbf(upar, tf)
                fuse_tok = tf
                if halo:
                    hpar = 6 if par == 0 else 7
                    tgh = act(ar[:, AR_FUSEH + m * 8: AR_FUSEH + (m + 1) * 8],
                              P[hpar][:, 0:8], AF.Silu,
                              scale=cf[:, CF_SGATE + m:CF_SGATE + m + 1], wait=[tGh])
                    tfh = vop("scalar_tensor_tensor",
                              ar[:, AR_FUSEH + m * 8: AR_FUSEH + (m + 1) * 8],
                              P[hpar][:, 8:16], cf[:, CF_SUP + m:CF_SUP + m + 1],
                              ar[:, AR_FUSEH + m * 8: AR_FUSEH + (m + 1) * 8],
                              wait=[tUh, tgh], op0=ALU.mult, op1=ALU.mult)
                    setbf(hpar, tgh, tfh)
                    fuse_tok = tfh
            slot_use["gu"] = gu_used[MC - 1]
            dn_used = [None] * (2 * OC)
            x_tok = None
            xc = [None] * OC
            pend_stats = [None]
            for o in range(OC):
                par = o % 2
                htoks = []
                for half in range(2):
                    hs = (2 * o + half) % 4
                    wsem = "W%d" % hs
                    wwait = later(w_chain[wsem],
                                  slot_use["z"] if o < 2 else None,
                                  slot_use["wout"] if o < 2 else None,
                                  slot_use["qkv"] if o < 2 else None)
                    htoks.append(dma("gpsimd",
                                     ws[:, WS_DN + hs * 4096: WS_DN + (hs + 1) * 4096],
                                     wdn[o, :, half * 4096:(half + 1) * 4096],
                                     wsem, wait=wwait))
                tD = None
                for mi in range(MC):
                    half = mi // 32
                    hs = (2 * o + half) % 4
                    tD = mm(P[par],
                            ws[:, WS_DN + hs * 4096 + (mi % 32) * 128: WS_DN + hs * 4096 + (mi % 32 + 1) * 128],
                            ar[:, AR_FUSE + mi * T: AR_FUSE + (mi + 1) * T],
                            mi == 0, mi == MC - 1,
                            wait=later(htoks[half] if mi % 32 == 0 else None,
                                       fuse_tok if mi == 0 else None,
                                       *bf(par, mi == 0)),
                            inc=((("PE", 1),) if mi in (31, MC - 1) else ()))
                    if mi == 31:
                        dn_used[2 * o] = tD
                        w_chain["W%d" % ((2 * o) % 4)] = tD
                dn_used[2 * o + 1] = tD
                w_chain["W%d" % ((2 * o + 1) % 4)] = tD
                tDh = None
                if halo:
                    hpar = 2 if par == 0 else 5
                    for mi in range(MC):
                        half = mi // 32
                        hs = (2 * o + half) % 4
                        tDh = mm(P[hpar][:, 0:8],
                                 ws[:, WS_DN + hs * 4096 + (mi % 32) * 128: WS_DN + hs * 4096 + (mi % 32 + 1) * 128],
                                 ar[:, AR_FUSEH + mi * 8: AR_FUSEH + (mi + 1) * 8],
                                 mi == 0, mi == MC - 1,
                                 wait=later(*bf(hpar, mi == 0)),
                                 inc=((("PE", 1),) if mi in (31, MC - 1) else ()))
                        if mi == 31:
                            dn_used[2 * o] = tDh
                            w_chain["W%d" % ((2 * o) % 4)] = tDh
                    dn_used[2 * o + 1] = tDh
                    w_chain["W%d" % ((2 * o + 1) % 4)] = tDh
                tx = vop("scalar_tensor_tensor", xt_c(o), P[par],
                         cf[:, CF_SDH + o:CF_SDH + o + 1], xt_c(o),
                         wait=later(tD, tok_sdh), op0=ALU.mult, op1=ALU.add)
                setbf(par, tx)
                x_tok = tx
                xc[o] = tx
                if halo:
                    hpar = 2 if par == 0 else 5
                    txh = vop("scalar_tensor_tensor", xh_c(o), P[hpar][:, 0:8],
                              cf[:, CF_SDH + o:CF_SDH + o + 1], xh_c(o),
                              wait=[tDh], op0=ALU.mult, op1=ALU.add)
                    setbf(hpar, txh)
                    x_tok = txh
                    xc[o] = txh
                if next_stats is not None:
                    if pend_stats[0] is not None:
                        next_stats["cb"](*pend_stats[0])
                    pend_stats[0] = (o, xc[o])
            if next_stats is not None and pend_stats[0] is not None:
                next_stats["cb"](*pend_stats[0])
            slot_use["dn"] = dn_used[2 * OC - 1]
            arena_tok[0] = slot_use["dn"]
            return x_tok, xc

        def conv_col(base, h, j):
            return cf[:, base + h * 4 + j: base + h * 4 + j + 1]

        def emit_ATTN(ln_toks, ln_tok_h, p, next_stats=None):
            halo = (p == 0)
            ta = tb = tah = None
            for k in range(KC):
                ta = mm(P2[0:16, :], ws[:, WS_WAB + k * 32: WS_WAB + k * 32 + 16],
                        xln_c(k), k == 0, k == KC - 1,
                        wait=later(tok_wab, ln_toks[k], *bf(2, k == 0)),
                        inc=((("PE", 1),) if k == KC - 1 else ()))
            for k in range(KC):
                tb = mm(P5[0:16, :], ws[:, WS_WAB + k * 32 + 16: WS_WAB + k * 32 + 32],
                        xln_c(k), k == 0, k == KC - 1,
                        wait=later(*bf(5, k == 0)),
                        inc=((("PE", 1),) if k == KC - 1 else ()))
            if halo:
                for k in range(KC):
                    tah = mm(P6[0:16, 0:8], ws[:, WS_WAB + k * 32: WS_WAB + k * 32 + 16],
                             xlnh_c(k), k == 0, k == KC - 1,
                             wait=later(ln_tok_h if k == 0 else None, *bf(6, k == 0)),
                             inc=((("PE", 1),) if k == KC - 1 else ()))
            if p > 0:
                tc1 = vop("tensor_copy", rws(RW_A, 0, 8), rws(RW_A, 512, 520))
                tc2 = vop("tensor_copy", rws(RW_K2, 0, 8), rws(RW_K2, 512, 520),
                          wait=later(tile_free["k2row"]))
            else:
                tc1, tc2 = tok_z1, tok_z2
            tae = vop("tensor_copy", rws(RW_A, 8, TT), P2[0:16, :], wait=later(ta, tc1))
            if halo:
                tae = vop("tensor_copy", rws(RW_A, 0, 8), P6[0:16, 0:8], wait=later(tah, tae))
                setbf(6, tae)
            setbf(2, tae)
            gts = []
            for s in range(4):
                tg1 = vop("tensor_mul", rws(RW_S1, 8, 8 + T), rws(RW_A, 8 - s, TT - s),
                          P5[0:16, :],
                          wait=later(tae, tb, gts[-1] if s > 0 else None))
                gts.append(act(ar[0:16, AR_G4B + s * T: AR_G4B + (s + 1) * T],
                               rws(RW_S1, 8, 8 + T), AF.Sigmoid,
                               wait=later(tg1, arena_tok[0])))
            setbf(5, gts[-1])
            g4_tok = gts[-1]

            em.mark(f"attn{p}_loop1")
            qkv_used = [None] * H
            prev_ksw = None
            nk2h_evs = []
            dot_toks = [None] * 6
            convv_toks = [None] * H
            out_ready = [None] * H
            DMT2 = ((AR_DM, AR_DM1, AR_DM2, AR_DM3),
                    (AR_WBC0, AR_WBC1, AR_LT1, AR_SQ2))
            QQT = (AR_QQ, AR_QQB)
            KKT = (AR_KK, AR_KKB)
            pend = []
            pend_late = [None]

            def flush_dots(specs):
                for kind, hh, s, tdm in specs:
                    pp = hh % 2
                    ho1_ = cb[:, CB_HO1 + hh * 16: CB_HO1 + (hh + 1) * 16]
                    hos_ = cb[:, CB_HOS + hh * 16: CB_HOS + (hh + 1) * 16]
                    if kind == "q2":
                        dot_toks[0] = mm(P2[0:16, :], ho1_, arr(QQT[pp], 0, T),
                                         hh == 0, hh == H - 1,
                                         wait=later(tdm, *bf(2, hh == 0)),
                                         inc=(("PE", 1),))
                        tile_free["qq%d" % pp] = dot_toks[0]
                    elif kind == "k2":
                        dot_toks[1] = mm(P3[0:16, :], ho1_, arr(KKT[pp], 0, T),
                                         hh == 0, hh == H - 1,
                                         wait=later(tdm, *bf(3, hh == 0)),
                                         inc=(("PE", 1),))
                        tile_free["kk%d" % pp] = dot_toks[1]
                    else:
                        dbank = 4 + s
                        dot_toks[2 + s] = mm(P[dbank][0:16, :], hos_,
                                             arr(DMT2[pp][s], 0, T), hh == 0, hh == H - 1,
                                             wait=later(tdm, *bf(dbank, hh == 0)),
                                             inc=(("PE", 1),))
                        tile_free["dm%d%d" % (s, pp)] = dot_toks[2 + s]

            for h in range(H):
                par = h % 2
                slot = h % 3
                wsem = "W%d" % slot
                wwait = later(w_chain[wsem],
                              slot_use["gu"] if h < 3 else None,
                              slot_use["dn"] if h < 3 else None,
                              slot_use["z"] if h < 3 else None,
                              slot_use["wout"] if h < 3 else None)
                wtok = dma("gpsimd", ws[:, WS_QKV + slot * 6144: WS_QKV + (slot + 1) * 6144],
                           wqkv[h, :, :], wsem, wait=wwait)
                wtok_q = wtok_k = wtok

                def qkvslc(w_i, k, pp=slot):
                    b = WS_QKV + pp * 6144 + w_i * 2048 + k * 128
                    return ws[:, b:b + 128]
                A, B = (P0, P1)
                SQ_, SK_, SV_ = ((AR_SQ, AR_SK, AR_SV) if par == 0
                                 else (AR_SQB, AR_SKB, AR_SVB))
                tq = tk = tv = None
                for k in range(KC):
                    tq = mm(A, qkvslc(0, k), xln_c(k), k == 0, k == KC - 1,
                            wait=later(wtok_q, ln_toks[k], g4_tok if h == 0 else None,
                                       *bf(0, k == 0)),
                            inc=((("PE", 1),) if k == KC - 1 else ()))
                for k in range(KC):
                    tk = mm(B, qkvslc(1, k), xln_c(k), k == 0, k == KC - 1,
                            wait=later(wtok_k if k == 0 else None, *bf(1, k == 0)),
                            inc=((("PE", 1),) if k == KC - 1 else ()))
                evq = act(arr(SQ_, 8, TT), A, AF.Silu,
                          wait=later(tq, arena_tok[0], tile_free["sq%d" % par]))
                for k in range(KC):
                    tv = mm(A, qkvslc(2, k), xln_c(k), k == 0, k == KC - 1,
                            wait=later(evq if k == 0 else None, wtok if k == 0 else None),
                            inc=((("PE", 1),) if k == KC - 1 else ()))
                if len(pend) == 2:
                    flush_dots(pend.pop(0))
                evk = act(arr(SK_, 8, TT), B, AF.Silu,
                          wait=later(tk, tile_free["sk%d" % par]))
                th = None
                if halo:
                    for w_i in range(3):
                        for k in range(KC):
                            th = mm(B[:, w_i * 8:(w_i + 1) * 8], qkvslc(w_i, k), xlnh_c(k),
                                    k == 0, k == KC - 1,
                                    wait=later((evk if (w_i == 0 and k == 0) else None),
                                               (ln_tok_h if (w_i == 0 and k == 0) else None)),
                                    inc=((("PE", 1),) if k == KC - 1 else ()))
                evv = act(arr(SV_, 8, TT), A, AF.Silu,
                          wait=later(tv, tile_free["sv%d" % par]))
                setbf(0, evv)
                if halo:
                    e1 = act(arr(SQ_, 0, 8), B[:, 0:8], AF.Silu, wait=[th])
                    e2 = act(arr(SK_, 0, 8), B[:, 8:16], AF.Silu, wait=[th])
                    e3 = act(arr(SV_, 0, 8), B[:, 16:24], AF.Silu, wait=[th])
                    fq, fk, fv = e1, e2, e3
                else:
                    fk = act(arr(SK_, 0, 8), carSk[:, h * 8:(h + 1) * 8], AF.Copy)
                    fq = act(arr(SQ_, 0, 8), carSq[:, h * 8:(h + 1) * 8], AF.Copy)
                    fv = act(arr(SV_, 0, 8), carSv[:, h * 8:(h + 1) * 8], AF.Copy)
                if pend_late[0] is not None:
                    pend_late[0]()
                    pend_late[0] = None
                def conv(dst_base, dst_lo, dst_hi, src_base, wbase, waits):
                    t0 = vop("tensor_scalar", arr(dst_base, dst_lo, dst_hi),
                             arr(src_base, dst_lo - 3, dst_hi - 3),
                             conv_col(wbase, h, 0), None,
                             wait=waits, op0=ALU.mult)
                    tl = t0
                    for j in (1, 2, 3):
                        tl = vop("scalar_tensor_tensor", arr(dst_base, dst_lo, dst_hi),
                                 arr(src_base, dst_lo - (3 - j), dst_hi - (3 - j)),
                                 conv_col(wbase, h, j), arr(dst_base, dst_lo, dst_hi),
                                 wait=[tl], op0=ALU.mult, op1=ALU.add)
                    return tl
                tck = conv(AR_CK, 8, TT, SK_, CF_CONVK,
                           later(evk, fk, prev_ksw, tile_free["late_kk"]))
                if halo:
                    tckh = conv(AR_CK, 3, 8, SK_, CF_CONVK, later(fk, prev_ksw))
                    vop("tensor_copy", arr(AR_CKH, 5 * h, 5 * h + 5), arr(AR_CK, 3, 8),
                        wait=later(tckh))
                else:
                    tckh = act(arr(AR_CK, 0, 8), carCk[:, h * 8:(h + 1) * 8], AF.Copy,
                               wait=later(tile_free["ckrd"]))
                ks1 = em.emit("sync", lambda e: e.dma_start(
                    out=ar[0:64, AR_KSW:AR_KSW + TT], in_=ar[64:128, AR_CK:AR_CK + TT]),
                    wait=(("DV", em.cnt["DV"]), ("AC", em.cnt["AC"])), inc=(("SW", 16),))
                ks2 = em.emit("sync", lambda e: e.dma_start(
                    out=ar[64:128, AR_KSW:AR_KSW + TT], in_=ar[0:64, AR_CK:AR_CK + TT]),
                    inc=(("SW", 16),))
                kswap_tok = ks2
                prev_ksw = ks2
                tcq = conv(AR_CQ, 8, TT, SQ_, CF_CONVQ, later(evq, fq))
                tcv = conv(AR_CV + h * TT, 8, TT, SV_, CF_CONVV, later(evv, fv))
                if halo:
                    tcvh = conv(AR_CV + h * TT, 3, 8, SV_, CF_CONVV, [fv])
                else:
                    tcvh = act(arr(AR_CV + h * TT, 0, 8), carCv[:, h * 8:(h + 1) * 8],
                               AF.Copy, wait=later(convv_free[h]))
                convv_toks[h] = later(tcv, tcvh)
                tile_free["sq%d" % par] = [tcq, act(carSq[:, h * 8:(h + 1) * 8],
                                                    arr(SQ_, 512, 520), AF.Copy)]
                tile_free["sk%d" % par] = [tck, act(carSk[:, h * 8:(h + 1) * 8],
                                                    arr(SK_, 512, 520), AF.Copy)]
                tile_free["sv%d" % par] = [tcv, act(carSv[:, h * 8:(h + 1) * 8],
                                                    arr(SV_, 512, 520), AF.Copy)]
                specs = []
                qq_ref, kk_ref = [None], [None]

                def late_ops(h=h, par=par, tck=tck, tcq=tcq, tcv=tcv,
                             qq_ref=qq_ref, kk_ref=kk_ref):
                    act(carCk[:, h * 8:(h + 1) * 8], arr(AR_CK, 512, 520), AF.Copy,
                        wait=[tck])
                    act(carCv[:, h * 8:(h + 1) * 8], arr(AR_CV + h * TT, 512, 520),
                        AF.Copy, wait=[tcv])
                    qq_ref[0] = act(arr(QQT[par], 0, T), arr(AR_CQ, 8, TT), AF.Square,
                                    wait=later(tcq, tile_free["qq%d" % par]))
                    kk_ref[0] = act(arr(KKT[par], 0, T), arr(AR_CK, 8, TT), AF.Square,
                                    wait=later(tck, tile_free["kk%d" % par]))
                    tile_free["late_kk"] = kk_ref[0]
                pend_late[0] = late_ops
                specs.append(("q2", h, 0, qq_ref))
                specs.append(("k2", h, 0, kk_ref))
                ho1 = cb[:, CB_HO1 + h * 16: CB_HO1 + (h + 1) * 16]
                for s in range(4):
                    if s == 0:
                        tdm = vop("tensor_mul", arr(DMT2[par][0], 0, T), arr(AR_CQ, 8, TT),
                                  arr(AR_CK, 8, TT),
                                  wait=later(tcq, tck, tckh, tile_free["dm0%d" % par]))
                    else:
                        tr1 = vop("tensor_scalar", arr(AR_LT0, 0, TT, (0, 128)),
                                  arr(AR_KSW, 0, TT),
                                  cf[:, CF_SINP + s - 1:CF_SINP + s], None,
                                  wait=later(kswap_tok), op0=ALU.mult)
                        tr2 = vop("scalar_tensor_tensor", arr(AR_RK, 0, TT),
                                  arr(AR_CK, 0, TT), cf[:, CF_COS + s - 1:CF_COS + s],
                                  arr(AR_LT0, 0, TT, (0, 128)),
                                  wait=later(tr1, tckh if s == 1 else None),
                                  op0=ALU.mult, op1=ALU.add)
                        tile_free["ckrd"] = tr2
                        tdm = vop("tensor_mul", arr(DMT2[par][s], 0, T), arr(AR_CQ, 8, TT),
                                  arr(AR_RK, 8 - s, TT - s),
                                  wait=later(tr2, tile_free["dm%d%d" % (s, par)]))
                    specs.append(("d", h, s, tdm))
                pend.append(specs)
                if halo:
                    setbf(1, e3)
                else:
                    setbf(1, evk)
                qkv_used[h] = th if halo else tv
                w_chain[wsem] = qkv_used[h]
            if pend_late[0] is not None:
                pend_late[0]()
                pend_late[0] = None
            while pend:
                flush_dots(pend.pop(0))
            slot_use["qkv"] = qkv_used[H - 1]
            nk2h_tok = None
            if halo:
                tsqh = vop("tensor_mul", arr(AR_CKH, 0, 80), arr(AR_CKH, 0, 80),
                           arr(AR_CKH, 0, 80), wait=later(*nk2h_evs))
                tnk = None
                for h2 in range(H):
                    tnk = mm(P0[0:16, 0:5], cb[:, CB_HO1 + h2 * 16: CB_HO1 + (h2 + 1) * 16],
                             arr(AR_CKH, 5 * h2, 5 * h2 + 5), h2 == 0, h2 == H - 1,
                             wait=later(tsqh, *bf(0, h2 == 0)),
                             inc=((("PE", 1),) if h2 == H - 1 else ()))
                nk2h_tok = vop("tensor_add", rows[0:16, RW_K2 + 3:RW_K2 + 8],
                               rows[0:16, RW_K2 + 3:RW_K2 + 8],
                               P0[0:16, 0:5], wait=[tnk, tok_z2])
                setbf(0, nk2h_tok)
                nk2h_evs = [nk2h_tok]

            em.mark(f"attn{p}_rows")
            tk2 = vop("tensor_copy", rws(RW_K2, 8, TT), P3[0:16, :],
                      wait=later(dot_toks[1], tc2, *nk2h_evs))
            setbf(3, tk2)
            t1 = act(rws(RW_S1, 0, T), P2[0:16, :], AF.Sqrt,
                     wait=later(dot_toks[0], gts[-1]))
            setbf(2, t1)
            t2 = vop("tensor_scalar_max", rws(RW_S1, 0, T), rws(RW_S1, 0, T), 1e-12, wait=[t1])
            trq = vop("reciprocal", rws(RW_RQ, 0, T), rws(RW_S1, 0, T), wait=[t2])
            t3 = act(rws(RW_S3, 0, TT), rws(RW_K2, 0, TT), AF.Sqrt, wait=[tk2])
            tile_free["k2row"] = t3
            t4 = vop("tensor_scalar_max", rws(RW_S3, 0, TT), rws(RW_S3, 0, TT), 1e-12, wait=[t3])
            trk = vop("reciprocal", rws(RW_RK, 0, TT), rws(RW_S3, 0, TT), wait=[t4])
            w4_toks = []
            for s in range(4):
                tw = vop("tensor_mul", rws(RW_S1, 0, T), P[4 + s][0:16, :],
                         ar[0:16, AR_G4B + s * T: AR_G4B + (s + 1) * T],
                         wait=later(dot_toks[2 + s], trq))
                setbf(4 + s, tw)
                tw = vop("tensor_mul", rws(RW_S1, 0, T), rws(RW_S1, 0, T), rws(RW_RQ, 0, T),
                         wait=[tw])
                tw = vop("tensor_mul", rws(RW_S1, 0, T), rws(RW_S1, 0, T),
                         rws(RW_RK, 8 - s, TT - s), wait=later(tw, trk))
                w4_toks.append(vop("tensor_copy",
                                   ar[0:16, AR_W4B + s * T: AR_W4B + (s + 1) * T],
                                   rws(RW_S1, 0, T), wait=[tw]))

            em.mark(f"attn{p}_loop2")
            z_used = [None] * H
            tsz_toks = [None] * H
            rms_tok = None
            pend_rms = [None]
            for h in range(H):
                par = h % 2
                wsem = "W0" if par == 0 else "W1"
                wwait = later(w_chain[wsem])
                wtok = dma("gpsimd", ws[:, WS_Z + par * 2048: WS_Z + (par + 1) * 2048],
                           wz[h, :, :], wsem, wait=wwait)
                zbank = 2 if par == 0 else 5
                tz = None
                for k in range(KC):
                    tz = mm(P[zbank], ws[:, WS_Z + par * 2048 + k * 128: WS_Z + par * 2048 + (k + 1) * 128],
                            xln_c(k), k == 0, k == KC - 1,
                            wait=later(wtok, *bf(zbank, k == 0)),
                            inc=((("PE", 1),) if k == KC - 1 else ()))
                z_used[h] = tz
                w_chain[wsem] = tz
                tsz = act(siluz[:, h * T:(h + 1) * T], P[zbank], AF.Silu, wait=[tz])
                tsz_toks[h] = tsz
                setbf(zbank, tsz)
                eh = cb[0:16, CB_EH + h * 128: CB_EH + (h + 1) * 128]
                mts = []
                for s in range(4):
                    wb = (0, 1, 6, 7)[s]
                    tb_ = mm(P[wb], eh, ar[0:16, AR_W4B + s * T: AR_W4B + (s + 1) * T],
                             True, True,
                             wait=later(w4_toks[s], *bf(wb)), inc=(("PE", 1),))
                    dst = (AR_DM, AR_DM1, AR_DM2, AR_DM3)[s]
                    mts.append(vop("tensor_mul", arr(dst, 0, T),
                                   P[wb],
                                   arr(AR_CV + h * TT, 8 - s, TT - s),
                                   wait=later(tb_, convv_toks[h],
                                              tile_free["dm%d0" % s])))
                    setbf(wb, mts[-1])
                    tile_free["dm%d0" % s] = None
                convv_free[h] = mts[3]
                s01 = vop("tensor_add", arr(AR_DM, 0, T), arr(AR_DM, 0, T), arr(AR_DM1, 0, T),
                          wait=later(mts[0], mts[1]))
                s23 = vop("tensor_add", arr(AR_DM2, 0, T), arr(AR_DM2, 0, T), arr(AR_DM3, 0, T),
                          wait=later(mts[2], mts[3]))
                tout = vop("tensor_add", arr(AR_OUT + h * T, 0, T), arr(AR_DM, 0, T),
                           arr(AR_DM2, 0, T), wait=later(s01, s23))
                out_ready[h] = tout
                if pend_rms[0] is not None:
                    hh, tt = pend_rms[0]
                    tsq_ = act(arr(AR_QQ, 0, T), arr(AR_OUT + hh * T, 0, T), AF.Square,
                               wait=later(tt, tile_free["qq"]))
                    rms_tok = mm(P3[0:16, :], cb[:, CB_HO1 + hh * 16: CB_HO1 + (hh + 1) * 16],
                                 arr(AR_QQ, 0, T), hh == 0, hh == H - 1,
                                 wait=later(tsq_, *bf(3, hh == 0)),
                                 inc=(("PE", 1),))
                    tile_free["qq"] = rms_tok
                pend_rms[0] = (h, tout)
            hh, tt = pend_rms[0]
            tsq_ = act(arr(AR_QQ, 0, T), arr(AR_OUT + hh * T, 0, T), AF.Square,
                       wait=later(tt, tile_free["qq"]))
            rms_tok = mm(P3[0:16, :], cb[:, CB_HO1 + hh * 16: CB_HO1 + (hh + 1) * 16],
                         arr(AR_QQ, 0, T), hh == 0, hh == H - 1,
                         wait=later(tsq_, *bf(3, hh == 0)),
                         inc=(("PE", 1),))
            tile_free["qq"] = rms_tok
            pend_rms[0] = None
            slot_use["z"] = z_used[H - 1]

            em.mark(f"attn{p}_rmsf")
            trm = vop("tensor_copy", rws(RW_S1, 0, T), P3[0:16, :], wait=[rms_tok])
            setbf(3, trm)
            tsum = mm(P4[0:1, :], cf[0:16, CF_ONE16:CF_ONE16 + 1], rws(RW_S1, 0, T),
                      True, True, wait=later(trm, *bf(4)), inc=(("PE", 1),))
            tr1_ = vop("tensor_copy", rows[0:1, RW_S3:RW_S3 + T], P4[0:1, :], wait=[tsum])
            setbf(4, tr1_)
            tr2_ = act(rows[0:1, RW_S1:RW_S1 + T], rows[0:1, RW_S3:RW_S3 + T], AF.Sqrt,
                       scale=cf[0:1, CF_INVD:CF_INVD + 1],
                       bias=cf[0:1, CF_RMSEPS:CF_RMSEPS + 1], wait=[tr1_])
            tr3_ = vop("reciprocal", rows[0:1, RW_S3:RW_S3 + T], rows[0:1, RW_S1:RW_S1 + T],
                       wait=[tr2_])
            tbc = mm(P4, cf[0:1, CF_ONEBC:CF_ONEBC + 128], rows[0:1, RW_S3:RW_S3 + T],
                     True, True, wait=later(tr3_, *bf(4)), inc=(("PE", 1),))

            em.mark(f"attn{p}_l3a")
            l3 = []
            for h in range(H):
                t1_ = vop("scalar_tensor_tensor", arr(AR_QQ, 0, T), arr(AR_OUT + h * T, 0, T),
                          cf[:, CF_NORMW + h:CF_NORMW + h + 1], P4,
                          wait=later(tbc, out_ready[h],
                                     tile_free["qq"] if h == 0 else None),
                          op0=ALU.mult, op1=ALU.mult)
                l3.append(vop("tensor_mul", siluz[:, h * T:(h + 1) * T],
                              siluz[:, h * T:(h + 1) * T], arr(AR_QQ, 0, T),
                              wait=later(t1_, tsz_toks[h])))
            setbf(4, l3[-1])
            win_tok = later(l3[-1])

            em.mark(f"attn{p}_l3b")
            wo_used = [None] * OC
            x_tok = None
            xc = [None] * OC
            pend_stats = [None]
            for o in range(OC):
                par = o % 2
                wsem = "W0" if par == 0 else "W1"
                wwait = later(w_chain[wsem],
                              slot_use["qkv"] if o < 2 else None)
                wtok = dma("gpsimd", ws[:, WS_WOUT + par * 2048: WS_WOUT + (par + 1) * 2048],
                           wou[o, :, :], wsem, wait=wwait)
                tW = None
                for k in range(KC):
                    tW = mm(P[par], ws[:, WS_WOUT + par * 2048 + k * 128: WS_WOUT + par * 2048 + (k + 1) * 128],
                            siluz[:, k * T:(k + 1) * T], k == 0, k == KC - 1,
                            wait=later(wtok, win_tok, *bf(par, k == 0)),
                            inc=((("PE", 1),) if k == KC - 1 else ()))
                wo_used[o] = tW
                w_chain[wsem] = tW
                tx = vop("tensor_add", xt_c(o), xt_c(o), P[par], wait=[tW])
                setbf(par, tx)
                x_tok = tx
                xc[o] = tx
                if next_stats is not None:
                    if pend_stats[0] is not None:
                        next_stats["cb"](*pend_stats[0])
                    pend_stats[0] = (o, tx)
            if next_stats is not None and pend_stats[0] is not None:
                next_stats["cb"](*pend_stats[0])
            slot_use["wout"] = wo_used[OC - 1]
            arena_tok[0] = x_tok
            return x_tok, xc

        store_cnt = 0
        st1_next = [None]

        def mk_post2(p, st1n):
            load_tok = [None] * KC

            def cbk(o, tx):
                dma("sync", yTr[o, :, p * T:(p + 1) * T], xt_c(o), "ST",
                    wait=[tx, ("ST", em.cnt["ST"])])
                if p + 1 < NPASS:
                    load_tok[o] = dma("sync", xt_c(o),
                                      xTr[o, :, (p + 1) * T:(p + 2) * T], "LX",
                                      wait=[("ST", em.cnt["ST"]),
                                            ("LX", em.cnt["LX"])])
                    if o >= 1 and st1n is not None:
                        st1n["cb"](o - 1, load_tok[o - 1])
            return cbk, load_tok

        for p in range(NPASS):
            halo = (p == 0)
            if halo:
                xl = []
                for c in range(KC):
                    xl.append(dma("sync", xt_c(c), xTr[c, :, p * T:(p + 1) * T],
                                  "LX"))
                xt0 = later(xl[-1], tok_xh, tok_cf, tok_cb)
                em.mark(f"p{p}_LN1")
                ln1, ln1h = emit_LN(lambda c: xt_c(c), lambda c: xh_c(c),
                                    lambda c: xln_c(c), lambda c: xlnh_c(c),
                                    lambda c: xt0, halo, f"ln1p{p}")
            else:
                em.mark(f"p{p}_LN1")
                ln1, ln1h = emit_LN(lambda c: xt_c(c), lambda c: xh_c(c),
                                    lambda c: xln_c(c), lambda c: xlnh_c(c),
                                    lambda c: None, False, f"ln1p{p}",
                                    stats=st1_next[0])
            em.mark(f"p{p}_FFN1")
            st2 = None if halo else ln_stats(lambda c: xt_c(c), single=True)
            xf, xf_c = emit_FFN(ln1, ln1h, halo, next_stats=st2)
            em.mark(f"p{p}_LN2")
            ln2, ln2h = emit_LN(lambda c: xt_c(c), lambda c: xh_c(c),
                                lambda c: xln_c(c), lambda c: xlnh_c(c),
                                lambda c: xf_c[c], halo, f"ln2p{p}", stats=st2)
            em.mark(f"p{p}_ATTN")
            st3 = ln_stats(lambda c: xt_c(c), single=True)
            xa, xa_c = emit_ATTN(ln2, ln2h, p, next_stats=st3)
            em.mark(f"p{p}_LN3")
            ln3, _ = emit_LN(lambda c: xt_c(c), lambda c: xh_c(c),
                             lambda c: xln_c(c), lambda c: xlnh_c(c),
                             lambda c: xa_c[c], False, f"ln3p{p}", stats=st3)
            em.mark(f"p{p}_FFN2")
            st1n = ln_stats(lambda c: xt_c(c), single=True) if p + 1 < NPASS else None
            post2, load_tok = mk_post2(p, st1n)
            xf2, xf2_c = emit_FFN(ln3, None, False, next_stats={"cb": post2})
            if st1n is not None:
                st1n["cb"](KC - 1, load_tok[KC - 1])
            st1_next[0] = st1n

        em.emit("sync", lambda e: e.nop(), wait=[("ST", em.cnt["ST"])])

        def run_ops(eng_obj, name):
            last = {}
            rec = build_program._inst_names.setdefault(name, [])
            for fn, waits, incs in em.ops[name]:
                for s, v in waits:
                    if last.get(s, 0) < v:
                        eng_obj.wait_ge(sems[s], v)
                        last[s] = v
                inst = fn(eng_obj)
                rec.append(getattr(inst.ins, "name", None))
                for s, n in incs:
                    inst.then_inc(sems[s], n)

        import json as _json
        build_program._inst_names = {}
        build_program._marks = em.marks
        build_program._n_tensor = len(em.ops["tensor"])

        @block.sync
        def _(e):
            run_ops(e, "sync")

        @block.gpsimd
        def _(e):
            run_ops(e, "gpsimd")

        @block.tensor
        def _(e):
            run_ops(e, "tensor")

        @block.vector
        def _(e):
            run_ops(e, "vector")

        @block.scalar
        def _(e):
            run_ops(e, "scalar")

    return nc


def _host_prep(inputs):
    f32 = np.float32
    Wq = np.asarray(inputs["L0_attn_Wq"], f32)
    Wk = np.asarray(inputs["L0_attn_Wk"], f32)
    Wv = np.asarray(inputs["L0_attn_Wv"], f32)
    Wz = np.asarray(inputs["L0_attn_Wz"], f32)
    Wa = np.asarray(inputs["L0_attn_Wa"], f32)
    Wb = np.asarray(inputs["L0_attn_Wb"], f32)
    Wout = np.asarray(inputs["L0_attn_Wout"], f32)
    w_gate = np.asarray(inputs["L0_ffn_w_gate"], np.int8)
    w_up = np.asarray(inputs["L0_ffn_w_up"], np.int8)
    w_down = np.asarray(inputs["L0_ffn_w_down"], np.int8)

    arr = np.stack([w_gate, w_up], axis=0).reshape(2, KC, 128, MC, 128)
    wgu = np.ascontiguousarray(arr.transpose(3, 2, 0, 1, 4).reshape(MC, 128, 2 * KC * 128))
    wdn = np.ascontiguousarray(
        w_down.reshape(MC, 128, OC, 128).transpose(2, 1, 0, 3).reshape(OC, 128, MC * 128))
    a3 = np.stack([Wq, Wk, Wv], axis=0).reshape(3, KC, 128, OC, 128)
    wqkv = np.ascontiguousarray(a3.transpose(3, 2, 0, 1, 4).reshape(OC, 128, 3 * KC * 128))
    wz = np.ascontiguousarray(
        Wz.reshape(KC, 128, OC, 128).transpose(2, 1, 0, 3).reshape(OC, 128, KC * 128))
    wou = np.ascontiguousarray(
        Wout.reshape(KC, 128, OC, 128).transpose(2, 1, 0, 3).reshape(OC, 128, KC * 128))
    wab = np.ascontiguousarray(
        np.concatenate([Wa, Wb], axis=1).reshape(KC, 128, 32).transpose(1, 0, 2).reshape(128, KC * 32))

    cf = np.zeros((128, NCF), f32)
    cf[:, CF_SGATE:CF_SGATE + 64] = np.asarray(inputs["L0_ffn_s_gate"], f32).reshape(64, 128).T
    cf[:, CF_SUP:CF_SUP + 64] = np.asarray(inputs["L0_ffn_s_up"], f32).reshape(64, 128).T
    cf[:, CF_SDOWN:CF_SDOWN + 16] = np.asarray(inputs["L0_ffn_s_down"], f32).reshape(16, 128).T
    cf[:, CF_NORMW:CF_NORMW + 16] = np.asarray(inputs["L0_attn_norm_w"], f32).reshape(16, 128).T
    for base, name in ((CF_CONVQ, "L0_attn_conv_q_w"), (CF_CONVK, "L0_attn_conv_k_w"),
                       (CF_CONVV, "L0_attn_conv_v_w")):
        cw = np.asarray(inputs[name], f32).reshape(16, 128, 4).transpose(1, 0, 2).reshape(128, 64)
        cf[:, base:base + 64] = cw
    theta = (1.0 / 1e6 ** (np.arange(0, 128, 2, dtype=np.float64) / 128.0))
    th128 = np.concatenate([theta, theta])
    for s in (1, 2, 3):
        cf[:, CF_COS + s - 1] = np.cos(s * th128).astype(f32)
        sp = np.sin(s * th128).astype(f32)
        sp[64:] = -sp[64:]
        cf[:, CF_SINP + s - 1] = sp
    cf[:, CF_ONE] = 1.0
    cf[0, CF_MEANBC:CF_MEANBC + 128] = 1.0 / HID
    cf[0, CF_ONEBC:CF_ONEBC + 128] = 1.0
    cf[:, CF_ONE16] = 1.0
    cf[:, CF_LNEPS] = 1e-5
    cf[:, CF_RMSEPS] = 1e-6
    cf[:, CF_INVD] = 1.0 / HID

    cb = np.zeros((128, NCB), ml_dtypes.bfloat16)
    ho = np.zeros((128, 256), np.float32)
    for h in range(H):
        ho[:, h * 16 + h] = 1.0
    cb[:, CB_HO1:CB_HO1 + 256] = ho.astype(ml_dtypes.bfloat16)
    cb[:, CB_HOS:CB_HOS + 256] = (ho / np.sqrt(128.0)).astype(ml_dtypes.bfloat16)
    cb[:, CB_ONECOL] = ml_dtypes.bfloat16(1.0)
    eh = np.zeros((128, 2048), np.float32)
    for h in range(H):
        eh[h, h * 128:(h + 1) * 128] = 1.0
    cb[:, CB_EH:CB_EH + 2048] = eh.astype(ml_dtypes.bfloat16)

    shared = {"wgu": wgu, "wdn": wdn, "wqkv": wqkv, "wz": wz, "wou": wou,
              "wab": wab, "cfd": cf, "cbd": cb}
    return shared


def kernel(**inputs):
    x = np.asarray(inputs["L0_attn_x"], np.float32)
    Bsz, S, _ = x.shape
    shared = _host_prep(inputs)
    in_maps = []
    for c in range(NCORE):
        b, half = c // 2, c % 2
        t0 = half * 2048
        sl = x[b, t0:t0 + 2048, :]
        if half == 0:
            hl = np.zeros((HALO, HID), np.float32)
        else:
            hl = x[b, t0 - HALO:t0, :]
        xT = np.ascontiguousarray(sl.T)
        hT = hl.T
        xh = np.ascontiguousarray(hT.reshape(KC, 128, HALO).transpose(1, 0, 2).reshape(128, KC * HALO))
        m = dict(shared)
        m["xT"] = xT
        m["xH"] = xh
        in_maps.append(m)
    nc = build_program()
    res = run_bass_kernel_spmd(nc, in_maps, core_ids=list(range(NCORE)))
    out = np.empty((Bsz, S, HID), np.float32)
    for c in range(NCORE):
        b, half = c // 2, c % 2
        out[b, half * 2048:(half + 1) * 2048, :] = res.results[c]["yT"].T
    return out


# version 20
# speedup vs baseline: 1.0446x; 1.0446x over previous
import numpy as np
import ml_dtypes
import concourse.bass as bass
import concourse.mybir as mybir
from concourse.bass_utils import run_bass_kernel_spmd

F32 = mybir.dt.float32
BF16 = mybir.dt.bfloat16
I8 = mybir.dt.int8
AF = mybir.ActivationFunctionType
ALU = mybir.AluOpType
AXX = mybir.AxisListType.X

T = 512
NPASS = 4
HALO = 8
TT = T + HALO
H = 16
KC = 16
MC = 64
OC = 16
HID = 2048
INTER = 8192
NCORE = 8

CF_SGATE = 0
CF_SUP = 64
CF_SDOWN = 128
CF_NORMW = 144
CF_CONVQ = 160
CF_CONVK = 224
CF_CONVV = 288
CF_COS = 352
CF_SINP = 355
CF_ONE = 358
CF_MEANBC = 360
CF_ONEBC = 488
CF_ONE16 = 616
CF_SDH = 617
CF_LNEPS = 633
CF_RMSEPS = 634
CF_INVD = 635
NCF = 640

CB_HO1 = 0
CB_HOS = 256
CB_ONECOL = 512
CB_EH = 516
NCB = 516 + 2048

WS_GU = 0
WS_DN = 8192
WS_QKV = 0
WS_Z = 24576
WS_WOUT = 0
WS_WAB = 28672
NWS = 29184

AR_FUSE = 0
AR_FUSEH = 32768
AR_CV = 0
AR_OUT = 8320
AR_SQ = 16512
AR_SK = 17032
AR_SV = 17552
AR_CQ = 18072
AR_CK = 18592
AR_KSW = 19112
AR_RK = 19632
AR_QQ = 20152
AR_KK = 20664
AR_DM = 21176
AR_WBC0 = 21688
AR_WBC1 = 22200
AR_LT0 = 22712
AR_LT1 = 23232
AR_W4B = 23752
AR_G4B = 25800
AR_SQ2 = 27848
AR_DM1 = 28896
AR_DM2 = 29408
AR_DM3 = 29920
AR_DMH = 30432
AR_SQB = 30440
AR_SKB = 30960
AR_SVB = 31480
AR_CKH = 32000
AR_QQB = 32080
AR_KKB = 32592
NAR = 33280

RW_K2 = 0
RW_A = 520
RW_S1 = 1040
RW_RQ = 1560
RW_RK = 2072
RW_S3 = 2592
NRW = 3112


class Em:
    ENGS = ("sync", "gpsimd", "tensor", "vector", "scalar")

    def __init__(self):
        self.ops = {e: [] for e in self.ENGS}
        self.cnt = {}
        self.marks = []

    def mark(self, name):
        self.marks.append((name, len(self.ops["tensor"])))

    def sem(self, name):
        self.cnt[name] = 0

    def emit(self, eng, fn, wait=(), inc=()):
        wait = tuple(t for t in wait if t is not None)
        self.ops[eng].append((fn, wait, tuple(inc)))
        toks = []
        for s, n in inc:
            self.cnt[s] += n
            toks.append((s, self.cnt[s]))
        return toks[0] if len(toks) == 1 else (toks if toks else None)


def later(*toks):
    out = []
    for t in toks:
        if t is None:
            continue
        if isinstance(t, list):
            out.extend(x for x in t if x is not None)
        else:
            out.append(t)
    return out


def build_program():
    nc = bass.Bass("TRN2", target_bir_lowering=False)

    xT = nc.dram_tensor("xT", [HID, NPASS * T], F32, kind="ExternalInput").ap()
    xH = nc.dram_tensor("xH", [128, KC * HALO], F32, kind="ExternalInput").ap()
    wgu = nc.dram_tensor("wgu", [MC, 128, 2 * KC * 128], I8, kind="ExternalInput").ap()
    wdn = nc.dram_tensor("wdn", [OC, 128, MC * 128], I8, kind="ExternalInput").ap()
    wqkv = nc.dram_tensor("wqkv", [OC, 128, 3 * KC * 128], F32, kind="ExternalInput").ap()
    wz = nc.dram_tensor("wz", [OC, 128, KC * 128], F32, kind="ExternalInput").ap()
    wou = nc.dram_tensor("wou", [OC, 128, KC * 128], F32, kind="ExternalInput").ap()
    wab = nc.dram_tensor("wab", [128, KC * 32], F32, kind="ExternalInput").ap()
    cfd = nc.dram_tensor("cfd", [128, NCF], F32, kind="ExternalInput").ap()
    cbd = nc.dram_tensor("cbd", [128, NCB], BF16, kind="ExternalInput").ap()
    yT = nc.dram_tensor("yT", [HID, NPASS * T], F32, kind="ExternalOutput").ap()
    xTr = xT.rearrange("(c p) t -> c p t", p=128)
    yTr = yT.rearrange("(c p) t -> c p t", p=128)

    em = Em()
    for s in ("LX", "ST", "SW", "W0", "W1", "W2", "W3", "WB", "PE", "DV", "AC", "GP"):
        em.sem(s)

    from contextlib import ExitStack
    with ExitStack() as _st:
        def _e(cm):
            return _st.enter_context(cm)
        xt = _e(nc.sbuf_tensor("xt", [128, KC * T], F32))
        xh = _e(nc.sbuf_tensor("xh", [128, KC * HALO], F32))
        xln = _e(nc.sbuf_tensor("xln", [128, KC * T], BF16))
        xlnh = _e(nc.sbuf_tensor("xlnh", [128, KC * HALO], BF16))
        siluz = _e(nc.sbuf_tensor("siluz", [128, KC * T], BF16))
        ar = _e(nc.sbuf_tensor("arena", [128, NAR], BF16))
        ws = _e(nc.sbuf_tensor("wst", [128, NWS], BF16))
        cf = _e(nc.sbuf_tensor("cf", [128, NCF], F32))
        cb = _e(nc.sbuf_tensor("cb", [128, NCB], BF16))
        rows = _e(nc.sbuf_tensor("rows", [128, NRW], F32))
        carSq = _e(nc.sbuf_tensor("carSq", [128, H * HALO], BF16))
        carSk = _e(nc.sbuf_tensor("carSk", [128, H * HALO], BF16))
        carSv = _e(nc.sbuf_tensor("carSv", [128, H * HALO], BF16))
        carCk = _e(nc.sbuf_tensor("carCk", [128, H * HALO], BF16))
        carCv = _e(nc.sbuf_tensor("carCv", [128, H * HALO], BF16))
        P0 = _e(nc.psum_tensor("P0", [128, T], F32))
        P1 = _e(nc.psum_tensor("P1", [128, T], F32))
        P2 = _e(nc.psum_tensor("P2", [128, T], F32))
        P3 = _e(nc.psum_tensor("P3", [128, T], F32))
        P4 = _e(nc.psum_tensor("P4", [128, T], F32))
        P5 = _e(nc.psum_tensor("P5", [128, T], F32))
        P6 = _e(nc.psum_tensor("P6", [128, T], F32))
        P7 = _e(nc.psum_tensor("P7", [128, T], F32))
        sLX = _e(nc.semaphore("LX"))
        sST = _e(nc.semaphore("ST"))
        sSW = _e(nc.semaphore("SW"))
        sW0 = _e(nc.semaphore("W0"))
        sW1 = _e(nc.semaphore("W1"))
        sPE = _e(nc.semaphore("PE"))
        sDV = _e(nc.semaphore("DV"))
        sAC = _e(nc.semaphore("AC"))
        sWB = _e(nc.semaphore("WB"))
        sGP = _e(nc.semaphore("GP"))
        sW2 = _e(nc.semaphore("W2"))
        sW3 = _e(nc.semaphore("W3"))
        block = _e(nc.Block())
        sems = {"LX": sLX, "ST": sST, "SW": sSW, "W0": sW0, "W1": sW1,
                "WB": sWB, "PE": sPE, "DV": sDV, "AC": sAC, "GP": sGP,
                "W2": sW2, "W3": sW3}
        P = [P0, P1, P2, P3, P4, P5, P6, P7]

        bankfree = [[] for _ in range(8)]

        def bf(i, cond=True):
            return list(bankfree[i]) if cond else []

        def setbf(i, *toks):
            bankfree[i] = [t for t in toks if t is not None]
        slot_use = {"gu": None, "dn": None, "qkv": None, "z": None, "wout": None}
        w_chain = {"W0": None, "W1": None, "W2": None, "W3": None}
        tile_free = {"qq": None, "kk": None, "sq0": None, "sk0": None,
                     "sv0": None, "sq1": None, "sk1": None, "sv1": None,
                     "k2row": None, "bcs": None, "ckrd": None, "late_kk": None}
        for _p in range(2):
            for _s in range(4):
                tile_free["dm%d%d" % (_s, _p)] = None
            tile_free["qq%d" % _p] = None
            tile_free["kk%d" % _p] = None
        arena_tok = [None]
        convv_free = [None] * H

        def _ap(x):
            if isinstance(x, bass.AP) or x is None or isinstance(x, (int, float)):
                return x
            try:
                return x[:, :]
            except Exception:
                return x

        def dma(q, out, in_, sem, wait=()):
            return em.emit(q, lambda e, o=out, i=in_: e.dma_start(out=o, in_=i),
                           wait=wait, inc=((sem, 16),))

        def mm(out, lhsT, rhs, start, stop, wait=(), inc=()):
            return em.emit("tensor",
                           lambda e, o=_ap(out), l=_ap(lhsT), r=_ap(rhs), st=start, sp=stop:
                           e.matmul(o, l, r, start=st, stop=sp),
                           wait=wait, inc=inc)

        def act(out, in_, func, wait=(), scale=None, bias=None, inc=(("AC", 1),)):
            wait = list(wait) + [("AC", em.cnt["AC"])]
            kw = {}
            if scale is not None:
                kw["scale"] = scale
            if bias is not None:
                kw["bias"] = bias
            return em.emit("scalar",
                           lambda e, o=_ap(out), i=_ap(in_), f=func, kw=kw: e.activation(o, i, f, **kw),
                           wait=wait, inc=inc)

        def gop(name, *args, wait=(), inc=(("GP", 1),), **kwargs):
            wait = list(wait) + [("GP", em.cnt["GP"])]
            args = tuple(_ap(a) for a in args)
            def fn(e, n=name, a=args, kw=kwargs):
                return getattr(e, n)(*a, **kw)
            return em.emit("gpsimd", fn, wait=wait, inc=inc)

        def vop(name, *args, wait=(), inc=(("DV", 1),), **kwargs):
            wait = list(wait) + [("DV", em.cnt["DV"])]
            args = tuple(_ap(a) for a in args)
            def fn(e, n=name, a=args, kw=kwargs):
                return getattr(e, n)(*a, **kw)
            return em.emit("vector", fn, wait=wait, inc=inc)

        def xt_c(c, lo=0, hi=T):
            return xt[:, c * T + lo: c * T + hi]

        def xln_c(c, lo=0, hi=T):
            return xln[:, c * T + lo: c * T + hi]

        def xh_c(c):
            return xh[:, c * HALO:(c + 1) * HALO]

        def xlnh_c(c):
            return xlnh[:, c * HALO:(c + 1) * HALO]

        def arr(base, lo, hi, parts=(0, 128)):
            return ar[parts[0]:parts[1], base + lo: base + hi]

        def rws(base, lo, hi, parts=(0, 16)):
            return rows[parts[0]:parts[1], base + lo: base + hi]

        dma("sync", cf[:, :], cfd[:, :], "LX")
        dma("sync", cb[:, :], cbd[:, :], "LX")
        tok3 = dma("sync", xh[:, :], xH[:, :], "LX")
        em.emit("sync", lambda e: e.nop(), wait=[tok3])
        tok_cf = tok_cb = tok_xh = tok3
        tok_wab = dma("gpsimd", ws[:, WS_WAB:WS_WAB + 512], wab[:, :], "WB", wait=())
        tok_sdh = vop("tensor_scalar", cf[:, CF_SDH:CF_SDH + 16],
                      cf[:, CF_SDOWN:CF_SDOWN + 16], 0.5, None,
                      wait=[tok_cf], op0=ALU.mult)
        tok_z1 = vop("memset", rows[0:16, RW_K2:RW_K2 + 8], 0.0)
        tok_z2 = vop("memset", rows[0:16, RW_A:RW_A + 8], 0.0)

        def ln_stats(src_c, single=False):
            if single:
                sqpar = [ar[:, AR_FUSEH:AR_FUSEH + T]] * 2
            else:
                sqpar = [ar[:, AR_SQ2 + i * T: AR_SQ2 + (i + 1) * T] for i in range(2)]
            st = {"sx": None, "sx2": None, "sqmm": [None, None]}

            def cbk(c, tok):
                par = c % 2
                a_t = act(sqpar[par], src_c(c), AF.Square,
                          wait=later(tok,
                                     st["sx2"] if single else st["sqmm"][par],
                                     arena_tok[0] if (not single and c == 0) else None))
                st["sx2"] = mm(P7[0:1, :], cb[:, CB_ONECOL:CB_ONECOL + 1], sqpar[par],
                               c == 0, c == KC - 1,
                               wait=later(a_t, *bf(7, c == 0)),
                               inc=(("PE", 1),))
                st["sqmm"][par] = st["sx2"]
                st["sx"] = mm(P6[0:1, :], cf[:, CF_ONE:CF_ONE + 1], src_c(c),
                              c == 0, c == KC - 1,
                              wait=later(tok, *bf(6, c == 0)),
                              inc=((("PE", 1),) if c == KC - 1 else ()))
            st["cb"] = cbk
            return st

        def emit_LN(src_c, src_h, out_ln, out_lnh, src_tok_c, halo, tag, stats=None):
            sqh = [ar[:, AR_SQ2 + 2 * T + i * HALO: AR_SQ2 + 2 * T + (i + 1) * HALO]
                   for i in range(2)]
            inphase = stats is None
            if inphase:
                stats = ln_stats(src_c)
                for c in range(KC):
                    stats["cb"](c, src_tok_c(c))
            tok_sx, tok_sx2 = stats["sx"], stats["sx2"]
            tok_stats_h = None
            if halo:
                sqh_tok = [None, None]
                for c in range(KC):
                    par = c % 2
                    a_t = act(sqh[par], src_h(c), AF.Square,
                              wait=later(src_tok_c(c), sqh_tok[par]))
                    sqh_tok[par] = mm(P2[0:1, 8:16], cb[:, CB_ONECOL:CB_ONECOL + 1],
                                      sqh[par], c == 0, c == KC - 1,
                                      wait=later(a_t, *bf(2, c == 0)),
                                      inc=(("PE", 1),))
                for c in range(KC):
                    tok_stats_h = mm(P2[0:1, 0:8], cf[:, CF_ONE:CF_ONE + 1], src_h(c),
                                     c == 0, c == KC - 1,
                                     inc=((("PE", 1),) if c == KC - 1 else ()))
            w = later(tok_sx, tok_sx2, tok_stats_h)
            tA = vop("tensor_copy", rows[0:1, RW_S1 + 8:RW_S1 + 8 + T], P6[0:1, :], wait=w)
            tB = vop("tensor_copy", rows[0:1, RW_S3 + 8:RW_S3 + 8 + T], P7[0:1, :], wait=w)
            if halo:
                tA = vop("tensor_copy", rows[0:1, RW_S1:RW_S1 + 8], P2[0:1, 0:8], wait=w)
                tB = vop("tensor_copy", rows[0:1, RW_S3:RW_S3 + 8], P2[0:1, 8:16], wait=w)
            setbf(6, tA)
            setbf(7, tB)
            setbf(2, tB)
            lo, hi = (0, TT) if halo else (8, TT)
            bcm = mm(P6, cf[0:1, CF_MEANBC:CF_MEANBC + 128],
                     rows[0:1, RW_S1 + 8:RW_S1 + 8 + T], True, True,
                     wait=later(tA, *bf(6)), inc=(("PE", 1),))
            bcmh = None
            if halo:
                bcmh = mm(P5[:, 0:8], cf[0:1, CF_MEANBC:CF_MEANBC + 128],
                          rows[0:1, RW_S1:RW_S1 + 8], True, True,
                          wait=later(tA, *bf(5)), inc=(("PE", 1),))
            t1 = vop("tensor_mul", rows[0:1, RW_S1 + lo:RW_S1 + hi],
                     rows[0:1, RW_S1 + lo:RW_S1 + hi], rows[0:1, RW_S1 + lo:RW_S1 + hi],
                     wait=later(tA, bcm, bcmh))
            tv = vop("scalar_tensor_tensor", rows[0:1, RW_S3 + lo:RW_S3 + hi],
                     rows[0:1, RW_S1 + lo:RW_S1 + hi], -1.0 / HID,
                     rows[0:1, RW_S3 + lo:RW_S3 + hi],
                     wait=[t1, tB], op0=ALU.mult, op1=ALU.add)
            tsq = act(rows[0:1, RW_S1 + lo:RW_S1 + hi], rows[0:1, RW_S3 + lo:RW_S3 + hi],
                      AF.Sqrt, scale=cf[0:1, CF_INVD:CF_INVD + 1],
                      bias=cf[0:1, CF_LNEPS:CF_LNEPS + 1], wait=[tv])
            tr = vop("reciprocal", rows[0:1, RW_S3 + lo:RW_S3 + hi],
                     rows[0:1, RW_S1 + lo:RW_S1 + hi], wait=[tsq])
            bcr = mm(P7, cf[0:1, CF_ONEBC:CF_ONEBC + 128],
                     rows[0:1, RW_S3 + 8:RW_S3 + 8 + T], True, True,
                     wait=later(tr, *bf(7)), inc=(("PE", 1),))
            bcrh = None
            if halo:
                bcrh = mm(P5[:, 8:16], cf[0:1, CF_ONEBC:CF_ONEBC + 128],
                          rows[0:1, RW_S3:RW_S3 + 8], True, True,
                          wait=[tr], inc=(("PE", 1),))
            toks = []
            for c in range(KC):
                t_s = vop("tensor_sub", out_ln(c), src_c(c), P6,
                          wait=later(bcm, bcr))
                toks.append(vop("tensor_mul", out_ln(c), out_ln(c), P7, wait=[t_s]))
            tok_h = None
            if halo:
                for c in range(KC):
                    t_s = vop("tensor_sub", out_lnh(c), src_h(c), P5[:, 0:8],
                              wait=later(bcmh, bcrh))
                    tok_h = vop("tensor_mul", out_lnh(c), out_lnh(c), P5[:, 8:16],
                                wait=[t_s])
                setbf(5, tok_h)
            setbf(6, toks[-1])
            setbf(7, toks[-1])
            if inphase:
                tile_free["dm31"] = tok_sx2
            return toks, tok_h

        def emit_FFN(ln_toks, ln_tok_h, halo, next_stats=None):
            gu_used = [None] * MC
            fuse_tok = None
            first_ffn_write = [later(arena_tok[0])]
            for m in range(MC):
                par = m % 2
                wsem = "W0" if par == 0 else "W1"
                wwait = later(w_chain[wsem],
                              slot_use["qkv"] if m < 2 else None,
                              slot_use["wout"] if m < 2 else None)
                wtok = dma("gpsimd", ws[:, WS_GU + par * 4096: WS_GU + (par + 1) * 4096],
                           wgu[m, :, :], wsem, wait=wwait)
                gslc = lambda k, pp=par: ws[:, WS_GU + pp * 4096 + k * 128: WS_GU + pp * 4096 + (k + 1) * 128]
                uslc = lambda k, pp=par: ws[:, WS_GU + pp * 4096 + 2048 + k * 128: WS_GU + pp * 4096 + 2048 + (k + 1) * 128]
                tG = tU = None
                for k in range(KC):
                    tG = mm(P[par], gslc(k), xln_c(k), k == 0, k == KC - 1,
                            wait=later(wtok, ln_toks[k], *bf(par, k == 0)),
                            inc=((("PE", 1),) if k == KC - 1 else ()))
                upar = 2 if par == 0 else 5
                for k in range(KC):
                    tU = mm(P[upar], uslc(k), xln_c(k), k == 0, k == KC - 1,
                            wait=later(*bf(upar, k == 0)),
                            inc=((("PE", 1),) if k == KC - 1 else ()))
                tGh = tUh = None
                if halo:
                    hpar = 6 if par == 0 else 7
                    for k in range(KC):
                        tGh = mm(P[hpar][:, 0:8], gslc(k), xlnh_c(k), k == 0, k == KC - 1,
                                 wait=later(ln_tok_h if k == 0 else None, *bf(hpar, k == 0)),
                                 inc=((("PE", 1),) if k == KC - 1 else ()))
                    for k in range(KC):
                        tUh = mm(P[hpar][:, 8:16], uslc(k), xlnh_c(k), k == 0, k == KC - 1,
                                 inc=((("PE", 1),) if k == KC - 1 else ()))
                gu_used[m] = tUh if halo else tU
                w_chain[wsem] = gu_used[m]
                tg = act(ar[:, AR_FUSE + m * T: AR_FUSE + (m + 1) * T], P[par], AF.Silu,
                         scale=cf[:, CF_SGATE + m:CF_SGATE + m + 1],
                         wait=later(tG, *first_ffn_write[0]))
                first_ffn_write[0] = []
                setbf(par, tg)
                tf = vop("scalar_tensor_tensor", ar[:, AR_FUSE + m * T: AR_FUSE + (m + 1) * T],
                         P[upar], cf[:, CF_SUP + m:CF_SUP + m + 1],
                         ar[:, AR_FUSE + m * T: AR_FUSE + (m + 1) * T],
                         wait=[tU, tg], op0=ALU.mult, op1=ALU.mult)
                setbf(upar, tf)
                fuse_tok = tf
                if halo:
                    hpar = 6 if par == 0 else 7
                    tgh = act(ar[:, AR_FUSEH + m * 8: AR_FUSEH + (m + 1) * 8],
                              P[hpar][:, 0:8], AF.Silu,
                              scale=cf[:, CF_SGATE + m:CF_SGATE + m + 1], wait=[tGh])
                    tfh = vop("scalar_tensor_tensor",
                              ar[:, AR_FUSEH + m * 8: AR_FUSEH + (m + 1) * 8],
                              P[hpar][:, 8:16], cf[:, CF_SUP + m:CF_SUP + m + 1],
                              ar[:, AR_FUSEH + m * 8: AR_FUSEH + (m + 1) * 8],
                              wait=[tUh, tgh], op0=ALU.mult, op1=ALU.mult)
                    setbf(hpar, tgh, tfh)
                    fuse_tok = tfh
            slot_use["gu"] = gu_used[MC - 1]
            dn_used = [None] * (2 * OC)
            x_tok = None
            xc = [None] * OC
            pend_stats = [None]
            for o in range(OC):
                par = o % 2
                htoks = []
                for half in range(2):
                    hs = (2 * o + half) % 4
                    wsem = "W%d" % hs
                    wwait = later(w_chain[wsem],
                                  slot_use["z"] if o < 2 else None,
                                  slot_use["wout"] if o < 2 else None,
                                  slot_use["qkv"] if o < 2 else None)
                    htoks.append(dma("gpsimd",
                                     ws[:, WS_DN + hs * 4096: WS_DN + (hs + 1) * 4096],
                                     wdn[o, :, half * 4096:(half + 1) * 4096],
                                     wsem, wait=wwait))
                tD = None
                for mi in range(MC):
                    half = mi // 32
                    hs = (2 * o + half) % 4
                    tD = mm(P[par],
                            ws[:, WS_DN + hs * 4096 + (mi % 32) * 128: WS_DN + hs * 4096 + (mi % 32 + 1) * 128],
                            ar[:, AR_FUSE + mi * T: AR_FUSE + (mi + 1) * T],
                            mi == 0, mi == MC - 1,
                            wait=later(htoks[half] if mi % 32 == 0 else None,
                                       fuse_tok if mi == 0 else None,
                                       *bf(par, mi == 0)),
                            inc=((("PE", 1),) if mi in (31, MC - 1) else ()))
                    if mi == 31:
                        dn_used[2 * o] = tD
                        w_chain["W%d" % ((2 * o) % 4)] = tD
                dn_used[2 * o + 1] = tD
                w_chain["W%d" % ((2 * o + 1) % 4)] = tD
                tDh = None
                if halo:
                    hpar = 2 if par == 0 else 5
                    for mi in range(MC):
                        half = mi // 32
                        hs = (2 * o + half) % 4
                        tDh = mm(P[hpar][:, 0:8],
                                 ws[:, WS_DN + hs * 4096 + (mi % 32) * 128: WS_DN + hs * 4096 + (mi % 32 + 1) * 128],
                                 ar[:, AR_FUSEH + mi * 8: AR_FUSEH + (mi + 1) * 8],
                                 mi == 0, mi == MC - 1,
                                 wait=later(*bf(hpar, mi == 0)),
                                 inc=((("PE", 1),) if mi in (31, MC - 1) else ()))
                        if mi == 31:
                            dn_used[2 * o] = tDh
                            w_chain["W%d" % ((2 * o) % 4)] = tDh
                    dn_used[2 * o + 1] = tDh
                    w_chain["W%d" % ((2 * o + 1) % 4)] = tDh
                tx = vop("scalar_tensor_tensor", xt_c(o), P[par],
                         cf[:, CF_SDH + o:CF_SDH + o + 1], xt_c(o),
                         wait=later(tD, tok_sdh), op0=ALU.mult, op1=ALU.add)
                setbf(par, tx)
                x_tok = tx
                xc[o] = tx
                if halo:
                    hpar = 2 if par == 0 else 5
                    txh = vop("scalar_tensor_tensor", xh_c(o), P[hpar][:, 0:8],
                              cf[:, CF_SDH + o:CF_SDH + o + 1], xh_c(o),
                              wait=[tDh], op0=ALU.mult, op1=ALU.add)
                    setbf(hpar, txh)
                    x_tok = txh
                    xc[o] = txh
                if next_stats is not None:
                    if pend_stats[0] is not None:
                        next_stats["cb"](*pend_stats[0])
                    pend_stats[0] = (o, xc[o])
            if next_stats is not None and pend_stats[0] is not None:
                next_stats["cb"](*pend_stats[0])
            slot_use["dn"] = dn_used[2 * OC - 1]
            arena_tok[0] = slot_use["dn"]
            return x_tok, xc

        def conv_col(base, h, j):
            return cf[:, base + h * 4 + j: base + h * 4 + j + 1]

        def emit_ATTN(ln_toks, ln_tok_h, p, next_stats=None):
            halo = (p == 0)
            ta = tb = tah = None
            for k in range(KC):
                ta = mm(P2[0:16, :], ws[:, WS_WAB + k * 32: WS_WAB + k * 32 + 16],
                        xln_c(k), k == 0, k == KC - 1,
                        wait=later(tok_wab, ln_toks[k], *bf(2, k == 0)),
                        inc=((("PE", 1),) if k == KC - 1 else ()))
            for k in range(KC):
                tb = mm(P5[0:16, :], ws[:, WS_WAB + k * 32 + 16: WS_WAB + k * 32 + 32],
                        xln_c(k), k == 0, k == KC - 1,
                        wait=later(*bf(5, k == 0)),
                        inc=((("PE", 1),) if k == KC - 1 else ()))
            if halo:
                for k in range(KC):
                    tah = mm(P6[0:16, 0:8], ws[:, WS_WAB + k * 32: WS_WAB + k * 32 + 16],
                             xlnh_c(k), k == 0, k == KC - 1,
                             wait=later(ln_tok_h if k == 0 else None, *bf(6, k == 0)),
                             inc=((("PE", 1),) if k == KC - 1 else ()))
            if p > 0:
                tc1 = vop("tensor_copy", rws(RW_A, 0, 8), rws(RW_A, 512, 520))
                tc2 = vop("tensor_copy", rws(RW_K2, 0, 8), rws(RW_K2, 512, 520),
                          wait=later(tile_free["k2row"]))
            else:
                tc1, tc2 = tok_z1, tok_z2
            tae = vop("tensor_copy", rws(RW_A, 8, TT), P2[0:16, :], wait=later(ta, tc1))
            if halo:
                tae = vop("tensor_copy", rws(RW_A, 0, 8), P6[0:16, 0:8], wait=later(tah, tae))
                setbf(6, tae)
            setbf(2, tae)
            gts = []
            for s in range(4):
                tg1 = vop("tensor_mul", rws(RW_S1, 8, 8 + T), rws(RW_A, 8 - s, TT - s),
                          P5[0:16, :],
                          wait=later(tae, tb, gts[-1] if s > 0 else None))
                gts.append(act(ar[0:16, AR_G4B + s * T: AR_G4B + (s + 1) * T],
                               rws(RW_S1, 8, 8 + T), AF.Sigmoid,
                               wait=later(tg1, arena_tok[0])))
            setbf(5, gts[-1])
            g4_tok = gts[-1]

            em.mark(f"attn{p}_loop1")
            qkv_used = [None] * H
            prev_ksw = None
            nk2h_evs = []
            dot_toks = [None] * 6
            convv_toks = [None] * H
            out_ready = [None] * H
            DMT2 = ((AR_DM, AR_DM1, AR_DM2, AR_DM3),
                    (AR_WBC0, AR_WBC1, AR_LT1, AR_SQ2))
            QQT = (AR_QQ, AR_QQB)
            KKT = (AR_KK, AR_KKB)
            pend = []
            pend_late = [None]

            def flush_dots(specs):
                for kind, hh, s, tdm in specs:
                    pp = hh % 2
                    ho1_ = cb[:, CB_HO1 + hh * 16: CB_HO1 + (hh + 1) * 16]
                    hos_ = cb[:, CB_HOS + hh * 16: CB_HOS + (hh + 1) * 16]
                    if kind == "q2":
                        dot_toks[0] = mm(P2[0:16, :], ho1_, arr(QQT[pp], 0, T),
                                         hh == 0, hh == H - 1,
                                         wait=later(tdm, *bf(2, hh == 0)),
                                         inc=(("PE", 1),))
                        tile_free["qq%d" % pp] = dot_toks[0]
                    elif kind == "k2":
                        dot_toks[1] = mm(P3[0:16, :], ho1_, arr(KKT[pp], 0, T),
                                         hh == 0, hh == H - 1,
                                         wait=later(tdm, *bf(3, hh == 0)),
                                         inc=(("PE", 1),))
                        tile_free["kk%d" % pp] = dot_toks[1]
                    else:
                        dbank = 4 + s
                        dot_toks[2 + s] = mm(P[dbank][0:16, :], hos_,
                                             arr(DMT2[pp][s], 0, T), hh == 0, hh == H - 1,
                                             wait=later(tdm, *bf(dbank, hh == 0)),
                                             inc=(("PE", 1),))
                        tile_free["dm%d%d" % (s, pp)] = dot_toks[2 + s]

            for h in range(H):
                par = h % 2
                slot = h % 3
                wsem = "W%d" % slot
                wwait = later(w_chain[wsem],
                              slot_use["gu"] if h < 3 else None,
                              slot_use["dn"] if h < 3 else None,
                              slot_use["z"] if h < 3 else None,
                              slot_use["wout"] if h < 3 else None)
                wtok = dma("gpsimd", ws[:, WS_QKV + slot * 6144: WS_QKV + (slot + 1) * 6144],
                           wqkv[h, :, :], wsem, wait=wwait)
                wtok_q = wtok_k = wtok

                def qkvslc(w_i, k, pp=slot):
                    b = WS_QKV + pp * 6144 + w_i * 2048 + k * 128
                    return ws[:, b:b + 128]
                A, B = (P0, P1)
                SQ_, SK_, SV_ = ((AR_SQ, AR_SK, AR_SV) if par == 0
                                 else (AR_SQB, AR_SKB, AR_SVB))
                tq = tk = tv = None
                for k in range(KC):
                    tq = mm(A, qkvslc(0, k), xln_c(k), k == 0, k == KC - 1,
                            wait=later(wtok_q, ln_toks[k], g4_tok if h == 0 else None,
                                       *bf(0, k == 0)),
                            inc=((("PE", 1),) if k == KC - 1 else ()))
                for k in range(KC):
                    tk = mm(B, qkvslc(1, k), xln_c(k), k == 0, k == KC - 1,
                            wait=later(wtok_k if k == 0 else None, *bf(1, k == 0)),
                            inc=((("PE", 1),) if k == KC - 1 else ()))
                evq = act(arr(SQ_, 8, TT), A, AF.Silu,
                          wait=later(tq, arena_tok[0], tile_free["sq%d" % par]))
                for k in range(KC):
                    tv = mm(A, qkvslc(2, k), xln_c(k), k == 0, k == KC - 1,
                            wait=later(evq if k == 0 else None, wtok if k == 0 else None),
                            inc=((("PE", 1),) if k == KC - 1 else ()))
                if len(pend) == 2:
                    flush_dots(pend.pop(0))
                evk = act(arr(SK_, 8, TT), B, AF.Silu,
                          wait=later(tk, tile_free["sk%d" % par]))
                th = None
                if halo:
                    for w_i in range(3):
                        for k in range(KC):
                            th = mm(B[:, w_i * 8:(w_i + 1) * 8], qkvslc(w_i, k), xlnh_c(k),
                                    k == 0, k == KC - 1,
                                    wait=later((evk if (w_i == 0 and k == 0) else None),
                                               (ln_tok_h if (w_i == 0 and k == 0) else None)),
                                    inc=((("PE", 1),) if k == KC - 1 else ()))
                evv = act(arr(SV_, 8, TT), A, AF.Silu,
                          wait=later(tv, tile_free["sv%d" % par]))
                setbf(0, evv)
                if halo:
                    e1 = act(arr(SQ_, 0, 8), B[:, 0:8], AF.Silu, wait=[th])
                    e2 = act(arr(SK_, 0, 8), B[:, 8:16], AF.Silu, wait=[th])
                    e3 = act(arr(SV_, 0, 8), B[:, 16:24], AF.Silu, wait=[th])
                    fq, fk, fv = e1, e2, e3
                else:
                    fk = act(arr(SK_, 0, 8), carSk[:, h * 8:(h + 1) * 8], AF.Copy)
                    fq = act(arr(SQ_, 0, 8), carSq[:, h * 8:(h + 1) * 8], AF.Copy)
                    fv = act(arr(SV_, 0, 8), carSv[:, h * 8:(h + 1) * 8], AF.Copy)
                if pend_late[0] is not None:
                    pend_late[0]()
                    pend_late[0] = None
                def conv(dst_base, dst_lo, dst_hi, src_base, wbase, waits):
                    t0 = vop("tensor_scalar", arr(dst_base, dst_lo, dst_hi),
                             arr(src_base, dst_lo - 3, dst_hi - 3),
                             conv_col(wbase, h, 0), None,
                             wait=waits, op0=ALU.mult)
                    tl = t0
                    for j in (1, 2, 3):
                        tl = vop("scalar_tensor_tensor", arr(dst_base, dst_lo, dst_hi),
                                 arr(src_base, dst_lo - (3 - j), dst_hi - (3 - j)),
                                 conv_col(wbase, h, j), arr(dst_base, dst_lo, dst_hi),
                                 wait=[tl], op0=ALU.mult, op1=ALU.add)
                    return tl
                tck = conv(AR_CK, 8, TT, SK_, CF_CONVK,
                           later(evk, fk, prev_ksw, tile_free["late_kk"]))
                if halo:
                    tckh = conv(AR_CK, 3, 8, SK_, CF_CONVK, later(fk, prev_ksw))
                    vop("tensor_copy", arr(AR_CKH, 5 * h, 5 * h + 5), arr(AR_CK, 3, 8),
                        wait=later(tckh))
                else:
                    tckh = act(arr(AR_CK, 0, 8), carCk[:, h * 8:(h + 1) * 8], AF.Copy,
                               wait=later(tile_free["ckrd"]))
                ks1 = em.emit("sync", lambda e: e.dma_start(
                    out=ar[0:64, AR_KSW:AR_KSW + TT], in_=ar[64:128, AR_CK:AR_CK + TT]),
                    wait=(("DV", em.cnt["DV"]), ("AC", em.cnt["AC"])), inc=(("SW", 16),))
                ks2 = em.emit("sync", lambda e: e.dma_start(
                    out=ar[64:128, AR_KSW:AR_KSW + TT], in_=ar[0:64, AR_CK:AR_CK + TT]),
                    inc=(("SW", 16),))
                kswap_tok = ks2
                prev_ksw = ks2
                tcq = conv(AR_CQ, 8, TT, SQ_, CF_CONVQ, later(evq, fq))
                tcv = conv(AR_CV + h * TT, 8, TT, SV_, CF_CONVV, later(evv, fv))
                if halo:
                    tcvh = conv(AR_CV + h * TT, 3, 8, SV_, CF_CONVV, [fv])
                else:
                    tcvh = act(arr(AR_CV + h * TT, 0, 8), carCv[:, h * 8:(h + 1) * 8],
                               AF.Copy, wait=later(convv_free[h]))
                convv_toks[h] = later(tcv, tcvh)
                tile_free["sq%d" % par] = [tcq, act(carSq[:, h * 8:(h + 1) * 8],
                                                    arr(SQ_, 512, 520), AF.Copy)]
                tile_free["sk%d" % par] = [tck, act(carSk[:, h * 8:(h + 1) * 8],
                                                    arr(SK_, 512, 520), AF.Copy)]
                tile_free["sv%d" % par] = [tcv, act(carSv[:, h * 8:(h + 1) * 8],
                                                    arr(SV_, 512, 520), AF.Copy)]
                specs = []
                qq_ref, kk_ref = [None], [None]

                def late_ops(h=h, par=par, tck=tck, tcq=tcq, tcv=tcv,
                             qq_ref=qq_ref, kk_ref=kk_ref):
                    act(carCk[:, h * 8:(h + 1) * 8], arr(AR_CK, 512, 520), AF.Copy,
                        wait=[tck])
                    act(carCv[:, h * 8:(h + 1) * 8], arr(AR_CV + h * TT, 512, 520),
                        AF.Copy, wait=[tcv])
                    qq_ref[0] = act(arr(QQT[par], 0, T), arr(AR_CQ, 8, TT), AF.Square,
                                    wait=later(tcq, tile_free["qq%d" % par]))
                    kk_ref[0] = act(arr(KKT[par], 0, T), arr(AR_CK, 8, TT), AF.Square,
                                    wait=later(tck, tile_free["kk%d" % par]))
                    tile_free["late_kk"] = kk_ref[0]
                pend_late[0] = late_ops
                specs.append(("q2", h, 0, qq_ref))
                specs.append(("k2", h, 0, kk_ref))
                ho1 = cb[:, CB_HO1 + h * 16: CB_HO1 + (h + 1) * 16]
                for s in range(4):
                    if s == 0:
                        tdm = vop("tensor_mul", arr(DMT2[par][0], 0, T), arr(AR_CQ, 8, TT),
                                  arr(AR_CK, 8, TT),
                                  wait=later(tcq, tck, tckh, tile_free["dm0%d" % par]))
                    else:
                        tr1 = vop("tensor_scalar", arr(AR_LT0, 0, TT, (0, 128)),
                                  arr(AR_KSW, 0, TT),
                                  cf[:, CF_SINP + s - 1:CF_SINP + s], None,
                                  wait=later(kswap_tok), op0=ALU.mult)
                        tr2 = vop("scalar_tensor_tensor", arr(AR_RK, 0, TT),
                                  arr(AR_CK, 0, TT), cf[:, CF_COS + s - 1:CF_COS + s],
                                  arr(AR_LT0, 0, TT, (0, 128)),
                                  wait=later(tr1, tckh if s == 1 else None),
                                  op0=ALU.mult, op1=ALU.add)
                        tile_free["ckrd"] = tr2
                        tdm = vop("tensor_mul", arr(DMT2[par][s], 0, T), arr(AR_CQ, 8, TT),
                                  arr(AR_RK, 8 - s, TT - s),
                                  wait=later(tr2, tile_free["dm%d%d" % (s, par)]))
                    specs.append(("d", h, s, tdm))
                pend.append(specs)
                if halo:
                    setbf(1, e3)
                else:
                    setbf(1, evk)
                qkv_used[h] = th if halo else tv
                w_chain[wsem] = qkv_used[h]
            if pend_late[0] is not None:
                pend_late[0]()
                pend_late[0] = None
            while pend:
                flush_dots(pend.pop(0))
            slot_use["qkv"] = qkv_used[H - 1]
            nk2h_tok = None
            if halo:
                tsqh = vop("tensor_mul", arr(AR_CKH, 0, 80), arr(AR_CKH, 0, 80),
                           arr(AR_CKH, 0, 80), wait=later(*nk2h_evs))
                tnk = None
                for h2 in range(H):
                    tnk = mm(P0[0:16, 0:5], cb[:, CB_HO1 + h2 * 16: CB_HO1 + (h2 + 1) * 16],
                             arr(AR_CKH, 5 * h2, 5 * h2 + 5), h2 == 0, h2 == H - 1,
                             wait=later(tsqh, *bf(0, h2 == 0)),
                             inc=((("PE", 1),) if h2 == H - 1 else ()))
                nk2h_tok = vop("tensor_add", rows[0:16, RW_K2 + 3:RW_K2 + 8],
                               rows[0:16, RW_K2 + 3:RW_K2 + 8],
                               P0[0:16, 0:5], wait=[tnk, tok_z2])
                setbf(0, nk2h_tok)
                nk2h_evs = [nk2h_tok]

            em.mark(f"attn{p}_rows")
            tk2 = vop("tensor_copy", rws(RW_K2, 8, TT), P3[0:16, :],
                      wait=later(dot_toks[1], tc2, *nk2h_evs))
            setbf(3, tk2)
            t1 = act(rws(RW_S1, 0, T), P2[0:16, :], AF.Sqrt,
                     wait=later(dot_toks[0], gts[-1]))
            setbf(2, t1)
            t2 = vop("tensor_scalar_max", rws(RW_S1, 0, T), rws(RW_S1, 0, T), 1e-12, wait=[t1])
            trq = vop("reciprocal", rws(RW_RQ, 0, T), rws(RW_S1, 0, T), wait=[t2])
            t3 = act(rws(RW_S3, 0, TT), rws(RW_K2, 0, TT), AF.Sqrt, wait=[tk2])
            tile_free["k2row"] = t3
            t4 = vop("tensor_scalar_max", rws(RW_S3, 0, TT), rws(RW_S3, 0, TT), 1e-12, wait=[t3])
            trk = vop("reciprocal", rws(RW_RK, 0, TT), rws(RW_S3, 0, TT), wait=[t4])
            w4_toks = []
            for s in range(4):
                tw = vop("tensor_mul", rws(RW_S1, 0, T), P[4 + s][0:16, :],
                         ar[0:16, AR_G4B + s * T: AR_G4B + (s + 1) * T],
                         wait=later(dot_toks[2 + s], trq))
                setbf(4 + s, tw)
                tw = vop("tensor_mul", rws(RW_S1, 0, T), rws(RW_S1, 0, T), rws(RW_RQ, 0, T),
                         wait=[tw])
                tw = vop("tensor_mul", rws(RW_S1, 0, T), rws(RW_S1, 0, T),
                         rws(RW_RK, 8 - s, TT - s), wait=later(tw, trk))
                w4_toks.append(vop("tensor_copy",
                                   ar[0:16, AR_W4B + s * T: AR_W4B + (s + 1) * T],
                                   rws(RW_S1, 0, T), wait=[tw]))

            em.mark(f"attn{p}_loop2")
            z_used = [None] * H
            tsz_toks = [None] * H
            rms_tok = None
            pend_rms = [None]
            for h in range(H):
                par = h % 2
                wsem = "W0" if par == 0 else "W1"
                wwait = later(w_chain[wsem])
                wtok = dma("gpsimd", ws[:, WS_Z + par * 2048: WS_Z + (par + 1) * 2048],
                           wz[h, :, :], wsem, wait=wwait)
                zbank = 2 if par == 0 else 5
                tz = None
                for k in range(KC):
                    tz = mm(P[zbank], ws[:, WS_Z + par * 2048 + k * 128: WS_Z + par * 2048 + (k + 1) * 128],
                            xln_c(k), k == 0, k == KC - 1,
                            wait=later(wtok, *bf(zbank, k == 0)),
                            inc=((("PE", 1),) if k == KC - 1 else ()))
                z_used[h] = tz
                w_chain[wsem] = tz
                tsz = act(siluz[:, h * T:(h + 1) * T], P[zbank], AF.Silu, wait=[tz])
                tsz_toks[h] = tsz
                setbf(zbank, tsz)
                eh = cb[0:16, CB_EH + h * 128: CB_EH + (h + 1) * 128]
                mts = []
                for s in range(4):
                    wb = (0, 1, 6, 7)[s]
                    tb_ = mm(P[wb], eh, ar[0:16, AR_W4B + s * T: AR_W4B + (s + 1) * T],
                             True, True,
                             wait=later(w4_toks[s], *bf(wb)), inc=(("PE", 1),))
                    dst = (AR_DM, AR_DM1, AR_DM2, AR_DM3)[s]
                    mts.append(vop("tensor_mul", arr(dst, 0, T),
                                   P[wb],
                                   arr(AR_CV + h * TT, 8 - s, TT - s),
                                   wait=later(tb_, convv_toks[h],
                                              tile_free["dm%d0" % s])))
                    setbf(wb, mts[-1])
                    tile_free["dm%d0" % s] = None
                convv_free[h] = mts[3]
                s01 = vop("tensor_add", arr(AR_DM, 0, T), arr(AR_DM, 0, T), arr(AR_DM1, 0, T),
                          wait=later(mts[0], mts[1]))
                s23 = vop("tensor_add", arr(AR_DM2, 0, T), arr(AR_DM2, 0, T), arr(AR_DM3, 0, T),
                          wait=later(mts[2], mts[3]))
                tout = vop("tensor_add", arr(AR_OUT + h * T, 0, T), arr(AR_DM, 0, T),
                           arr(AR_DM2, 0, T), wait=later(s01, s23))
                out_ready[h] = tout
                if pend_rms[0] is not None:
                    hh, tt = pend_rms[0]
                    tsq_ = act(arr(AR_QQ, 0, T), arr(AR_OUT + hh * T, 0, T), AF.Square,
                               wait=later(tt, tile_free["qq"]))
                    rms_tok = mm(P3[0:16, :], cb[:, CB_HO1 + hh * 16: CB_HO1 + (hh + 1) * 16],
                                 arr(AR_QQ, 0, T), hh == 0, hh == H - 1,
                                 wait=later(tsq_, *bf(3, hh == 0)),
                                 inc=(("PE", 1),))
                    tile_free["qq"] = rms_tok
                pend_rms[0] = (h, tout)
            hh, tt = pend_rms[0]
            tsq_ = act(arr(AR_QQ, 0, T), arr(AR_OUT + hh * T, 0, T), AF.Square,
                       wait=later(tt, tile_free["qq"]))
            rms_tok = mm(P3[0:16, :], cb[:, CB_HO1 + hh * 16: CB_HO1 + (hh + 1) * 16],
                         arr(AR_QQ, 0, T), hh == 0, hh == H - 1,
                         wait=later(tsq_, *bf(3, hh == 0)),
                         inc=(("PE", 1),))
            tile_free["qq"] = rms_tok
            pend_rms[0] = None
            slot_use["z"] = z_used[H - 1]

            em.mark(f"attn{p}_rmsf")
            trm = vop("tensor_copy", rws(RW_S1, 0, T), P3[0:16, :], wait=[rms_tok])
            setbf(3, trm)
            tsum = mm(P4[0:1, :], cf[0:16, CF_ONE16:CF_ONE16 + 1], rws(RW_S1, 0, T),
                      True, True, wait=later(trm, *bf(4)), inc=(("PE", 1),))
            tr1_ = vop("tensor_copy", rows[0:1, RW_S3:RW_S3 + T], P4[0:1, :], wait=[tsum])
            setbf(4, tr1_)
            tr2_ = act(rows[0:1, RW_S1:RW_S1 + T], rows[0:1, RW_S3:RW_S3 + T], AF.Sqrt,
                       scale=cf[0:1, CF_INVD:CF_INVD + 1],
                       bias=cf[0:1, CF_RMSEPS:CF_RMSEPS + 1], wait=[tr1_])
            tr3_ = vop("reciprocal", rows[0:1, RW_S3:RW_S3 + T], rows[0:1, RW_S1:RW_S1 + T],
                       wait=[tr2_])
            tbc = mm(P4, cf[0:1, CF_ONEBC:CF_ONEBC + 128], rows[0:1, RW_S3:RW_S3 + T],
                     True, True, wait=later(tr3_, *bf(4)), inc=(("PE", 1),))

            em.mark(f"attn{p}_l3a")
            l3 = []
            for h in range(H):
                t1_ = vop("scalar_tensor_tensor", arr(AR_QQ, 0, T), arr(AR_OUT + h * T, 0, T),
                          cf[:, CF_NORMW + h:CF_NORMW + h + 1], P4,
                          wait=later(tbc, out_ready[h],
                                     tile_free["qq"] if h == 0 else None),
                          op0=ALU.mult, op1=ALU.mult)
                l3.append(vop("tensor_mul", siluz[:, h * T:(h + 1) * T],
                              siluz[:, h * T:(h + 1) * T], arr(AR_QQ, 0, T),
                              wait=later(t1_, tsz_toks[h])))
            setbf(4, l3[-1])
            win_tok = later(l3[-1])

            em.mark(f"attn{p}_l3b")
            wo_used = [None] * OC
            x_tok = None
            xc = [None] * OC
            pend_stats = [None]
            for o in range(OC):
                par = o % 2
                wsem = "W0" if par == 0 else "W1"
                wwait = later(w_chain[wsem],
                              slot_use["qkv"] if o < 2 else None)
                wtok = dma("gpsimd", ws[:, WS_WOUT + par * 2048: WS_WOUT + (par + 1) * 2048],
                           wou[o, :, :], wsem, wait=wwait)
                tW = None
                for k in range(KC):
                    tW = mm(P[par], ws[:, WS_WOUT + par * 2048 + k * 128: WS_WOUT + par * 2048 + (k + 1) * 128],
                            siluz[:, k * T:(k + 1) * T], k == 0, k == KC - 1,
                            wait=later(wtok, win_tok, *bf(par, k == 0)),
                            inc=((("PE", 1),) if k == KC - 1 else ()))
                wo_used[o] = tW
                w_chain[wsem] = tW
                tx = vop("tensor_add", xt_c(o), xt_c(o), P[par], wait=[tW])
                setbf(par, tx)
                x_tok = tx
                xc[o] = tx
                if next_stats is not None:
                    if pend_stats[0] is not None:
                        next_stats["cb"](*pend_stats[0])
                    pend_stats[0] = (o, tx)
            if next_stats is not None and pend_stats[0] is not None:
                next_stats["cb"](*pend_stats[0])
            slot_use["wout"] = wo_used[OC - 1]
            arena_tok[0] = x_tok
            return x_tok, xc

        store_cnt = 0
        st1_next = [None]

        def mk_post2(p, st1n):
            load_tok = [None] * KC

            def cbk(o, tx):
                dma("sync", yTr[o, :, p * T:(p + 1) * T], xt_c(o), "ST",
                    wait=[tx])
                if p + 1 < NPASS:
                    load_tok[o] = dma("sync", xt_c(o),
                                      xTr[o, :, (p + 1) * T:(p + 2) * T], "LX",
                                      wait=[("ST", em.cnt["ST"]),
                                            ("LX", em.cnt["LX"])])
                    if o >= 1 and st1n is not None:
                        st1n["cb"](o - 1, load_tok[o - 1])
            return cbk, load_tok

        for p in range(NPASS):
            halo = (p == 0)
            if halo:
                xl = []
                for c in range(KC):
                    xl.append(dma("sync", xt_c(c), xTr[c, :, p * T:(p + 1) * T],
                                  "LX"))
                xt0 = later(xl[-1], tok_xh, tok_cf, tok_cb)
                em.mark(f"p{p}_LN1")
                ln1, ln1h = emit_LN(lambda c: xt_c(c), lambda c: xh_c(c),
                                    lambda c: xln_c(c), lambda c: xlnh_c(c),
                                    lambda c: xt0, halo, f"ln1p{p}")
            else:
                em.mark(f"p{p}_LN1")
                ln1, ln1h = emit_LN(lambda c: xt_c(c), lambda c: xh_c(c),
                                    lambda c: xln_c(c), lambda c: xlnh_c(c),
                                    lambda c: None, False, f"ln1p{p}",
                                    stats=st1_next[0])
            em.mark(f"p{p}_FFN1")
            st2 = None if halo else ln_stats(lambda c: xt_c(c), single=True)
            xf, xf_c = emit_FFN(ln1, ln1h, halo, next_stats=st2)
            em.mark(f"p{p}_LN2")
            ln2, ln2h = emit_LN(lambda c: xt_c(c), lambda c: xh_c(c),
                                lambda c: xln_c(c), lambda c: xlnh_c(c),
                                lambda c: xf_c[c], halo, f"ln2p{p}", stats=st2)
            em.mark(f"p{p}_ATTN")
            st3 = ln_stats(lambda c: xt_c(c), single=True)
            xa, xa_c = emit_ATTN(ln2, ln2h, p, next_stats=st3)
            em.mark(f"p{p}_LN3")
            ln3, _ = emit_LN(lambda c: xt_c(c), lambda c: xh_c(c),
                             lambda c: xln_c(c), lambda c: xlnh_c(c),
                             lambda c: xa_c[c], False, f"ln3p{p}", stats=st3)
            em.mark(f"p{p}_FFN2")
            st1n = ln_stats(lambda c: xt_c(c), single=True) if p + 1 < NPASS else None
            post2, load_tok = mk_post2(p, st1n)
            xf2, xf2_c = emit_FFN(ln3, None, False, next_stats={"cb": post2})
            if st1n is not None:
                st1n["cb"](KC - 1, load_tok[KC - 1])
            st1_next[0] = st1n

        em.emit("sync", lambda e: e.nop(), wait=[("ST", em.cnt["ST"])])

        def run_ops(eng_obj, name):
            last = {}
            rec = build_program._inst_names.setdefault(name, [])
            for fn, waits, incs in em.ops[name]:
                for s, v in waits:
                    if last.get(s, 0) < v:
                        eng_obj.wait_ge(sems[s], v)
                        last[s] = v
                inst = fn(eng_obj)
                rec.append(getattr(inst.ins, "name", None))
                for s, n in incs:
                    inst.then_inc(sems[s], n)

        import json as _json
        build_program._inst_names = {}
        build_program._marks = em.marks
        build_program._n_tensor = len(em.ops["tensor"])

        @block.sync
        def _(e):
            run_ops(e, "sync")

        @block.gpsimd
        def _(e):
            run_ops(e, "gpsimd")

        @block.tensor
        def _(e):
            run_ops(e, "tensor")

        @block.vector
        def _(e):
            run_ops(e, "vector")

        @block.scalar
        def _(e):
            run_ops(e, "scalar")

    return nc


def _host_prep(inputs):
    f32 = np.float32
    Wq = np.asarray(inputs["L0_attn_Wq"], f32)
    Wk = np.asarray(inputs["L0_attn_Wk"], f32)
    Wv = np.asarray(inputs["L0_attn_Wv"], f32)
    Wz = np.asarray(inputs["L0_attn_Wz"], f32)
    Wa = np.asarray(inputs["L0_attn_Wa"], f32)
    Wb = np.asarray(inputs["L0_attn_Wb"], f32)
    Wout = np.asarray(inputs["L0_attn_Wout"], f32)
    w_gate = np.asarray(inputs["L0_ffn_w_gate"], np.int8)
    w_up = np.asarray(inputs["L0_ffn_w_up"], np.int8)
    w_down = np.asarray(inputs["L0_ffn_w_down"], np.int8)

    arr = np.stack([w_gate, w_up], axis=0).reshape(2, KC, 128, MC, 128)
    wgu = np.ascontiguousarray(arr.transpose(3, 2, 0, 1, 4).reshape(MC, 128, 2 * KC * 128))
    wdn = np.ascontiguousarray(
        w_down.reshape(MC, 128, OC, 128).transpose(2, 1, 0, 3).reshape(OC, 128, MC * 128))
    a3 = np.stack([Wq, Wk, Wv], axis=0).reshape(3, KC, 128, OC, 128)
    wqkv = np.ascontiguousarray(a3.transpose(3, 2, 0, 1, 4).reshape(OC, 128, 3 * KC * 128))
    wz = np.ascontiguousarray(
        Wz.reshape(KC, 128, OC, 128).transpose(2, 1, 0, 3).reshape(OC, 128, KC * 128))
    wou = np.ascontiguousarray(
        Wout.reshape(KC, 128, OC, 128).transpose(2, 1, 0, 3).reshape(OC, 128, KC * 128))
    wab = np.ascontiguousarray(
        np.concatenate([Wa, Wb], axis=1).reshape(KC, 128, 32).transpose(1, 0, 2).reshape(128, KC * 32))

    cf = np.zeros((128, NCF), f32)
    cf[:, CF_SGATE:CF_SGATE + 64] = np.asarray(inputs["L0_ffn_s_gate"], f32).reshape(64, 128).T
    cf[:, CF_SUP:CF_SUP + 64] = np.asarray(inputs["L0_ffn_s_up"], f32).reshape(64, 128).T
    cf[:, CF_SDOWN:CF_SDOWN + 16] = np.asarray(inputs["L0_ffn_s_down"], f32).reshape(16, 128).T
    cf[:, CF_NORMW:CF_NORMW + 16] = np.asarray(inputs["L0_attn_norm_w"], f32).reshape(16, 128).T
    for base, name in ((CF_CONVQ, "L0_attn_conv_q_w"), (CF_CONVK, "L0_attn_conv_k_w"),
                       (CF_CONVV, "L0_attn_conv_v_w")):
        cw = np.asarray(inputs[name], f32).reshape(16, 128, 4).transpose(1, 0, 2).reshape(128, 64)
        cf[:, base:base + 64] = cw
    theta = (1.0 / 1e6 ** (np.arange(0, 128, 2, dtype=np.float64) / 128.0))
    th128 = np.concatenate([theta, theta])
    for s in (1, 2, 3):
        cf[:, CF_COS + s - 1] = np.cos(s * th128).astype(f32)
        sp = np.sin(s * th128).astype(f32)
        sp[64:] = -sp[64:]
        cf[:, CF_SINP + s - 1] = sp
    cf[:, CF_ONE] = 1.0
    cf[0, CF_MEANBC:CF_MEANBC + 128] = 1.0 / HID
    cf[0, CF_ONEBC:CF_ONEBC + 128] = 1.0
    cf[:, CF_ONE16] = 1.0
    cf[:, CF_LNEPS] = 1e-5
    cf[:, CF_RMSEPS] = 1e-6
    cf[:, CF_INVD] = 1.0 / HID

    cb = np.zeros((128, NCB), ml_dtypes.bfloat16)
    ho = np.zeros((128, 256), np.float32)
    for h in range(H):
        ho[:, h * 16 + h] = 1.0
    cb[:, CB_HO1:CB_HO1 + 256] = ho.astype(ml_dtypes.bfloat16)
    cb[:, CB_HOS:CB_HOS + 256] = (ho / np.sqrt(128.0)).astype(ml_dtypes.bfloat16)
    cb[:, CB_ONECOL] = ml_dtypes.bfloat16(1.0)
    eh = np.zeros((128, 2048), np.float32)
    for h in range(H):
        eh[h, h * 128:(h + 1) * 128] = 1.0
    cb[:, CB_EH:CB_EH + 2048] = eh.astype(ml_dtypes.bfloat16)

    shared = {"wgu": wgu, "wdn": wdn, "wqkv": wqkv, "wz": wz, "wou": wou,
              "wab": wab, "cfd": cf, "cbd": cb}
    return shared


def kernel(**inputs):
    x = np.asarray(inputs["L0_attn_x"], np.float32)
    Bsz, S, _ = x.shape
    shared = _host_prep(inputs)
    in_maps = []
    for c in range(NCORE):
        b, half = c // 2, c % 2
        t0 = half * 2048
        sl = x[b, t0:t0 + 2048, :]
        if half == 0:
            hl = np.zeros((HALO, HID), np.float32)
        else:
            hl = x[b, t0 - HALO:t0, :]
        xT = np.ascontiguousarray(sl.T)
        hT = hl.T
        xh = np.ascontiguousarray(hT.reshape(KC, 128, HALO).transpose(1, 0, 2).reshape(128, KC * HALO))
        m = dict(shared)
        m["xT"] = xT
        m["xH"] = xh
        in_maps.append(m)
    nc = build_program()
    res = run_bass_kernel_spmd(nc, in_maps, core_ids=list(range(NCORE)))
    out = np.empty((Bsz, S, HID), np.float32)
    for c in range(NCORE):
        b, half = c // 2, c % 2
        out[b, half * 2048:(half + 1) * 2048, :] = res.results[c]["yT"].T
    return out


# version 21 (final)
# speedup vs baseline: 1.0831x; 1.0368x over previous
import numpy as np
import ml_dtypes
import concourse.bass as bass
import concourse.mybir as mybir
from concourse.bass_utils import run_bass_kernel_spmd

F32 = mybir.dt.float32
BF16 = mybir.dt.bfloat16
I8 = mybir.dt.int8
AF = mybir.ActivationFunctionType
ALU = mybir.AluOpType
AXX = mybir.AxisListType.X

T = 512
NPASS = 4
HALO = 8
TT = T + HALO
H = 16
KC = 16
MC = 64
OC = 16
HID = 2048
INTER = 8192
NCORE = 8

CF_SGATE = 0
CF_SUP = 64
CF_SDOWN = 128
CF_NORMW = 144
CF_CONVQ = 160
CF_CONVK = 224
CF_CONVV = 288
CF_COS = 352
CF_SINP = 355
CF_ONE = 358
CF_MEANBC = 360
CF_ONEBC = 488
CF_ONE16 = 616
CF_SDH = 617
CF_LNEPS = 633
CF_RMSEPS = 634
CF_INVD = 635
NCF = 640

CB_HO1 = 0
CB_HOS = 256
CB_ONECOL = 512
CB_EH = 516
NCB = 516 + 2048

WS_GU = 0
WS_DN = 8192
WS_QKV = 0
WS_Z = 24576
WS_WOUT = 0
WS_WAB = 28672
NWS = 29184

AR_FUSE = 0
AR_FUSEH = 32768
AR_CV = 0
AR_OUT = 8320
AR_SQ = 16512
AR_SK = 17032
AR_SV = 17552
AR_CQ = 18072
AR_CK = 18592
AR_KSW = 19112
AR_RK = 19632
AR_QQ = 20152
AR_KK = 20664
AR_DM = 21176
AR_WBC0 = 21688
AR_WBC1 = 22200
AR_LT0 = 22712
AR_LT1 = 23232
AR_W4B = 23752
AR_G4B = 25800
AR_SQ2 = 27848
AR_DM1 = 28896
AR_DM2 = 29408
AR_DM3 = 29920
AR_DMH = 30432
AR_SQB = 30440
AR_SKB = 30960
AR_SVB = 31480
AR_CKH = 32000
AR_QQB = 32080
AR_KKB = 32592
NAR = 33280

RW_K2 = 0
RW_A = 520
RW_S1 = 1040
RW_RQ = 1560
RW_RK = 2072
RW_S3 = 2592
NRW = 3112


class Em:
    ENGS = ("sync", "gpsimd", "tensor", "vector", "scalar")

    def __init__(self):
        self.ops = {e: [] for e in self.ENGS}
        self.cnt = {}
        self.marks = []

    def mark(self, name):
        self.marks.append((name, len(self.ops["tensor"])))

    def sem(self, name):
        self.cnt[name] = 0

    def emit(self, eng, fn, wait=(), inc=()):
        wait = tuple(t for t in wait if t is not None)
        self.ops[eng].append((fn, wait, tuple(inc)))
        toks = []
        for s, n in inc:
            self.cnt[s] += n
            toks.append((s, self.cnt[s]))
        return toks[0] if len(toks) == 1 else (toks if toks else None)


def later(*toks):
    out = []
    for t in toks:
        if t is None:
            continue
        if isinstance(t, list):
            out.extend(x for x in t if x is not None)
        else:
            out.append(t)
    return out


def build_program():
    nc = bass.Bass("TRN2", target_bir_lowering=False)

    xT = nc.dram_tensor("xT", [HID, NPASS * T], F32, kind="ExternalInput").ap()
    xH = nc.dram_tensor("xH", [128, KC * HALO], F32, kind="ExternalInput").ap()
    wgu = nc.dram_tensor("wgu", [MC, 128, 2 * KC * 128], I8, kind="ExternalInput").ap()
    wdn = nc.dram_tensor("wdn", [OC, 128, MC * 128], I8, kind="ExternalInput").ap()
    wqkv = nc.dram_tensor("wqkv", [OC, 128, 3 * KC * 128], F32, kind="ExternalInput").ap()
    wz = nc.dram_tensor("wz", [OC, 128, KC * 128], F32, kind="ExternalInput").ap()
    wou = nc.dram_tensor("wou", [OC, 128, KC * 128], F32, kind="ExternalInput").ap()
    wab = nc.dram_tensor("wab", [128, KC * 32], F32, kind="ExternalInput").ap()
    cfd = nc.dram_tensor("cfd", [128, NCF], F32, kind="ExternalInput").ap()
    cbd = nc.dram_tensor("cbd", [128, NCB], BF16, kind="ExternalInput").ap()
    yT = nc.dram_tensor("yT", [HID, NPASS * T], F32, kind="ExternalOutput").ap()
    xTr = xT.rearrange("(c p) t -> c p t", p=128)
    yTr = yT.rearrange("(c p) t -> c p t", p=128)

    em = Em()
    for s in ("LX", "ST", "SW", "W0", "W1", "W2", "W3", "WB", "PE", "DV", "AC", "GP"):
        em.sem(s)

    from contextlib import ExitStack
    with ExitStack() as _st:
        def _e(cm):
            return _st.enter_context(cm)
        xt = _e(nc.sbuf_tensor("xt", [128, KC * T], F32))
        xh = _e(nc.sbuf_tensor("xh", [128, KC * HALO], F32))
        xln = _e(nc.sbuf_tensor("xln", [128, KC * T], BF16))
        xlnh = _e(nc.sbuf_tensor("xlnh", [128, KC * HALO], BF16))
        siluz = _e(nc.sbuf_tensor("siluz", [128, KC * T], BF16))
        ar = _e(nc.sbuf_tensor("arena", [128, NAR], BF16))
        ws = _e(nc.sbuf_tensor("wst", [128, NWS], BF16))
        cf = _e(nc.sbuf_tensor("cf", [128, NCF], F32))
        cb = _e(nc.sbuf_tensor("cb", [128, NCB], BF16))
        rows = _e(nc.sbuf_tensor("rows", [128, NRW], F32))
        carSq = _e(nc.sbuf_tensor("carSq", [128, H * HALO], BF16))
        carSk = _e(nc.sbuf_tensor("carSk", [128, H * HALO], BF16))
        carSv = _e(nc.sbuf_tensor("carSv", [128, H * HALO], BF16))
        carCk = _e(nc.sbuf_tensor("carCk", [128, H * HALO], BF16))
        carCv = _e(nc.sbuf_tensor("carCv", [128, H * HALO], BF16))
        P0 = _e(nc.psum_tensor("P0", [128, T], F32))
        P1 = _e(nc.psum_tensor("P1", [128, T], F32))
        P2 = _e(nc.psum_tensor("P2", [128, T], F32))
        P3 = _e(nc.psum_tensor("P3", [128, T], F32))
        P4 = _e(nc.psum_tensor("P4", [128, T], F32))
        P5 = _e(nc.psum_tensor("P5", [128, T], F32))
        P6 = _e(nc.psum_tensor("P6", [128, T], F32))
        P7 = _e(nc.psum_tensor("P7", [128, T], F32))
        sLX = _e(nc.semaphore("LX"))
        sST = _e(nc.semaphore("ST"))
        sSW = _e(nc.semaphore("SW"))
        sW0 = _e(nc.semaphore("W0"))
        sW1 = _e(nc.semaphore("W1"))
        sPE = _e(nc.semaphore("PE"))
        sDV = _e(nc.semaphore("DV"))
        sAC = _e(nc.semaphore("AC"))
        sWB = _e(nc.semaphore("WB"))
        sGP = _e(nc.semaphore("GP"))
        sW2 = _e(nc.semaphore("W2"))
        sW3 = _e(nc.semaphore("W3"))
        block = _e(nc.Block())
        sems = {"LX": sLX, "ST": sST, "SW": sSW, "W0": sW0, "W1": sW1,
                "WB": sWB, "PE": sPE, "DV": sDV, "AC": sAC, "GP": sGP,
                "W2": sW2, "W3": sW3}
        P = [P0, P1, P2, P3, P4, P5, P6, P7]

        bankfree = [[] for _ in range(8)]

        def bf(i, cond=True):
            return list(bankfree[i]) if cond else []

        def setbf(i, *toks):
            bankfree[i] = [t for t in toks if t is not None]
        slot_use = {"gu": None, "dn": None, "qkv": None, "z": None, "wout": None}
        w_chain = {"W0": None, "W1": None, "W2": None, "W3": None}
        tile_free = {"qq": None, "kk": None, "sq0": None, "sk0": None,
                     "sv0": None, "sq1": None, "sk1": None, "sv1": None,
                     "k2row": None, "bcs": None, "ckrd": None, "late_kk": None}
        for _p in range(2):
            for _s in range(4):
                tile_free["dm%d%d" % (_s, _p)] = None
            tile_free["qq%d" % _p] = None
            tile_free["kk%d" % _p] = None
        arena_tok = [None]
        convv_free = [None] * H

        def _ap(x):
            if isinstance(x, bass.AP) or x is None or isinstance(x, (int, float)):
                return x
            try:
                return x[:, :]
            except Exception:
                return x

        def dma(q, out, in_, sem, wait=()):
            return em.emit(q, lambda e, o=out, i=in_: e.dma_start(out=o, in_=i),
                           wait=wait, inc=((sem, 16),))

        def mm(out, lhsT, rhs, start, stop, wait=(), inc=()):
            return em.emit("tensor",
                           lambda e, o=_ap(out), l=_ap(lhsT), r=_ap(rhs), st=start, sp=stop:
                           e.matmul(o, l, r, start=st, stop=sp),
                           wait=wait, inc=inc)

        def act(out, in_, func, wait=(), scale=None, bias=None, inc=(("AC", 1),)):
            wait = list(wait) + [("AC", em.cnt["AC"])]
            kw = {}
            if scale is not None:
                kw["scale"] = scale
            if bias is not None:
                kw["bias"] = bias
            return em.emit("scalar",
                           lambda e, o=_ap(out), i=_ap(in_), f=func, kw=kw: e.activation(o, i, f, **kw),
                           wait=wait, inc=inc)

        def gop(name, *args, wait=(), inc=(("GP", 1),), **kwargs):
            wait = list(wait) + [("GP", em.cnt["GP"])]
            args = tuple(_ap(a) for a in args)
            def fn(e, n=name, a=args, kw=kwargs):
                return getattr(e, n)(*a, **kw)
            return em.emit("gpsimd", fn, wait=wait, inc=inc)

        def vop(name, *args, wait=(), inc=(("DV", 1),), **kwargs):
            wait = list(wait) + [("DV", em.cnt["DV"])]
            args = tuple(_ap(a) for a in args)
            def fn(e, n=name, a=args, kw=kwargs):
                return getattr(e, n)(*a, **kw)
            return em.emit("vector", fn, wait=wait, inc=inc)

        def xt_c(c, lo=0, hi=T):
            return xt[:, c * T + lo: c * T + hi]

        def xln_c(c, lo=0, hi=T):
            return xln[:, c * T + lo: c * T + hi]

        def xh_c(c):
            return xh[:, c * HALO:(c + 1) * HALO]

        def xlnh_c(c):
            return xlnh[:, c * HALO:(c + 1) * HALO]

        def arr(base, lo, hi, parts=(0, 128)):
            return ar[parts[0]:parts[1], base + lo: base + hi]

        def rws(base, lo, hi, parts=(0, 16)):
            return rows[parts[0]:parts[1], base + lo: base + hi]

        dma("sync", cf[:, :], cfd[:, :], "LX")
        dma("sync", cb[:, :], cbd[:, :], "LX")
        tok3 = dma("sync", xh[:, :], xH[:, :], "LX")
        em.emit("sync", lambda e: e.nop(), wait=[tok3])
        tok_cf = tok_cb = tok_xh = tok3
        tok_wab = dma("gpsimd", ws[:, WS_WAB:WS_WAB + 512], wab[:, :], "WB", wait=())
        tok_sdh = vop("tensor_scalar", cf[:, CF_SDH:CF_SDH + 16],
                      cf[:, CF_SDOWN:CF_SDOWN + 16], 0.5, None,
                      wait=[tok_cf], op0=ALU.mult)
        tok_z1 = vop("memset", rows[0:16, RW_K2:RW_K2 + 8], 0.0)
        tok_z2 = vop("memset", rows[0:16, RW_A:RW_A + 8], 0.0)

        def ln_stats(src_c, single=False):
            if single:
                sqpar = [ar[:, AR_FUSEH:AR_FUSEH + T]] * 2
            else:
                sqpar = [ar[:, AR_SQ2 + i * T: AR_SQ2 + (i + 1) * T] for i in range(2)]
            st = {"sx": None, "sx2": None, "sqmm": [None, None]}

            def cbk(c, tok):
                par = c % 2
                a_t = act(sqpar[par], src_c(c), AF.Square,
                          wait=later(tok,
                                     st["sx2"] if single else st["sqmm"][par],
                                     arena_tok[0] if (not single and c == 0) else None))
                st["sx2"] = mm(P7[0:1, :], cb[:, CB_ONECOL:CB_ONECOL + 1], sqpar[par],
                               c == 0, c == KC - 1,
                               wait=later(a_t, *bf(7, c == 0)),
                               inc=(("PE", 1),))
                st["sqmm"][par] = st["sx2"]
                st["sx"] = mm(P6[0:1, :], cf[:, CF_ONE:CF_ONE + 1], src_c(c),
                              c == 0, c == KC - 1,
                              wait=later(tok, *bf(6, c == 0)),
                              inc=((("PE", 1),) if c == KC - 1 else ()))
            st["cb"] = cbk
            return st

        def emit_LN(src_c, src_h, out_ln, out_lnh, src_tok_c, halo, tag, stats=None):
            sqh = [ar[:, AR_SQ2 + 2 * T + i * HALO: AR_SQ2 + 2 * T + (i + 1) * HALO]
                   for i in range(2)]
            inphase = stats is None
            if inphase:
                stats = ln_stats(src_c)
                for c in range(KC):
                    stats["cb"](c, src_tok_c(c))
            tok_sx, tok_sx2 = stats["sx"], stats["sx2"]
            tok_stats_h = None
            if halo:
                sqh_tok = [None, None]
                for c in range(KC):
                    par = c % 2
                    a_t = act(sqh[par], src_h(c), AF.Square,
                              wait=later(src_tok_c(c), sqh_tok[par]))
                    sqh_tok[par] = mm(P2[0:1, 8:16], cb[:, CB_ONECOL:CB_ONECOL + 1],
                                      sqh[par], c == 0, c == KC - 1,
                                      wait=later(a_t, *bf(2, c == 0)),
                                      inc=(("PE", 1),))
                for c in range(KC):
                    tok_stats_h = mm(P2[0:1, 0:8], cf[:, CF_ONE:CF_ONE + 1], src_h(c),
                                     c == 0, c == KC - 1,
                                     inc=((("PE", 1),) if c == KC - 1 else ()))
            w = later(tok_sx, tok_sx2, tok_stats_h)
            tA = vop("tensor_copy", rows[0:1, RW_S1 + 8:RW_S1 + 8 + T], P6[0:1, :], wait=w)
            tB = vop("tensor_copy", rows[0:1, RW_S3 + 8:RW_S3 + 8 + T], P7[0:1, :], wait=w)
            if halo:
                tA = vop("tensor_copy", rows[0:1, RW_S1:RW_S1 + 8], P2[0:1, 0:8], wait=w)
                tB = vop("tensor_copy", rows[0:1, RW_S3:RW_S3 + 8], P2[0:1, 8:16], wait=w)
            setbf(6, tA)
            setbf(7, tB)
            setbf(2, tB)
            lo, hi = (0, TT) if halo else (8, TT)
            bcm = mm(P6, cf[0:1, CF_MEANBC:CF_MEANBC + 128],
                     rows[0:1, RW_S1 + 8:RW_S1 + 8 + T], True, True,
                     wait=later(tA, *bf(6)), inc=(("PE", 1),))
            bcmh = None
            if halo:
                bcmh = mm(P5[:, 0:8], cf[0:1, CF_MEANBC:CF_MEANBC + 128],
                          rows[0:1, RW_S1:RW_S1 + 8], True, True,
                          wait=later(tA, *bf(5)), inc=(("PE", 1),))
            t1 = vop("tensor_mul", rows[0:1, RW_S1 + lo:RW_S1 + hi],
                     rows[0:1, RW_S1 + lo:RW_S1 + hi], rows[0:1, RW_S1 + lo:RW_S1 + hi],
                     wait=later(tA, bcm, bcmh))
            tv = vop("scalar_tensor_tensor", rows[0:1, RW_S3 + lo:RW_S3 + hi],
                     rows[0:1, RW_S1 + lo:RW_S1 + hi], -1.0 / HID,
                     rows[0:1, RW_S3 + lo:RW_S3 + hi],
                     wait=[t1, tB], op0=ALU.mult, op1=ALU.add)
            tsq = act(rows[0:1, RW_S1 + lo:RW_S1 + hi], rows[0:1, RW_S3 + lo:RW_S3 + hi],
                      AF.Sqrt, scale=cf[0:1, CF_INVD:CF_INVD + 1],
                      bias=cf[0:1, CF_LNEPS:CF_LNEPS + 1], wait=[tv])
            tr = vop("reciprocal", rows[0:1, RW_S3 + lo:RW_S3 + hi],
                     rows[0:1, RW_S1 + lo:RW_S1 + hi], wait=[tsq])
            bcr = mm(P7, cf[0:1, CF_ONEBC:CF_ONEBC + 128],
                     rows[0:1, RW_S3 + 8:RW_S3 + 8 + T], True, True,
                     wait=later(tr, *bf(7)), inc=(("PE", 1),))
            bcrh = None
            if halo:
                bcrh = mm(P5[:, 8:16], cf[0:1, CF_ONEBC:CF_ONEBC + 128],
                          rows[0:1, RW_S3:RW_S3 + 8], True, True,
                          wait=[tr], inc=(("PE", 1),))
            toks = []
            for c in range(KC):
                t_s = vop("tensor_sub", out_ln(c), src_c(c), P6,
                          wait=later(bcm, bcr))
                toks.append(vop("tensor_mul", out_ln(c), out_ln(c), P7, wait=[t_s]))
            tok_h = None
            if halo:
                for c in range(KC):
                    t_s = vop("tensor_sub", out_lnh(c), src_h(c), P5[:, 0:8],
                              wait=later(bcmh, bcrh))
                    tok_h = vop("tensor_mul", out_lnh(c), out_lnh(c), P5[:, 8:16],
                                wait=[t_s])
                setbf(5, tok_h)
            setbf(6, toks[-1])
            setbf(7, toks[-1])
            if inphase:
                tile_free["dm31"] = tok_sx2
            return toks, tok_h

        def emit_FFN(ln_toks, ln_tok_h, halo, next_stats=None):
            gu_used = [None] * MC
            fuse_tok = None
            first_ffn_write = [later(arena_tok[0])]
            for m in range(MC):
                par = m % 2
                wsem = "W0" if par == 0 else "W1"
                wwait = later(w_chain[wsem],
                              slot_use["qkv"] if m < 2 else None,
                              slot_use["wout"] if m < 2 else None)
                wtok = dma("gpsimd", ws[:, WS_GU + par * 4096: WS_GU + (par + 1) * 4096],
                           wgu[m, :, :], wsem, wait=wwait)
                gslc = lambda k, pp=par: ws[:, WS_GU + pp * 4096 + k * 128: WS_GU + pp * 4096 + (k + 1) * 128]
                uslc = lambda k, pp=par: ws[:, WS_GU + pp * 4096 + 2048 + k * 128: WS_GU + pp * 4096 + 2048 + (k + 1) * 128]
                tG = tU = None
                for k in range(KC):
                    tG = mm(P[par], gslc(k), xln_c(k), k == 0, k == KC - 1,
                            wait=later(wtok, ln_toks[k], *bf(par, k == 0)),
                            inc=((("PE", 1),) if k == KC - 1 else ()))
                upar = 2 if par == 0 else 5
                for k in range(KC):
                    tU = mm(P[upar], uslc(k), xln_c(k), k == 0, k == KC - 1,
                            wait=later(*bf(upar, k == 0)),
                            inc=((("PE", 1),) if k == KC - 1 else ()))
                tGh = tUh = None
                if halo:
                    hpar = 6 if par == 0 else 7
                    for k in range(KC):
                        tGh = mm(P[hpar][:, 0:8], gslc(k), xlnh_c(k), k == 0, k == KC - 1,
                                 wait=later(ln_tok_h if k == 0 else None, *bf(hpar, k == 0)),
                                 inc=((("PE", 1),) if k == KC - 1 else ()))
                    for k in range(KC):
                        tUh = mm(P[hpar][:, 8:16], uslc(k), xlnh_c(k), k == 0, k == KC - 1,
                                 inc=((("PE", 1),) if k == KC - 1 else ()))
                gu_used[m] = tUh if halo else tU
                w_chain[wsem] = gu_used[m]
                tg = act(ar[:, AR_FUSE + m * T: AR_FUSE + (m + 1) * T], P[par], AF.Silu,
                         scale=cf[:, CF_SGATE + m:CF_SGATE + m + 1],
                         wait=later(tG, *first_ffn_write[0]))
                first_ffn_write[0] = []
                setbf(par, tg)
                tf = vop("scalar_tensor_tensor", ar[:, AR_FUSE + m * T: AR_FUSE + (m + 1) * T],
                         P[upar], cf[:, CF_SUP + m:CF_SUP + m + 1],
                         ar[:, AR_FUSE + m * T: AR_FUSE + (m + 1) * T],
                         wait=[tU, tg], op0=ALU.mult, op1=ALU.mult)
                setbf(upar, tf)
                fuse_tok = tf
                if halo:
                    hpar = 6 if par == 0 else 7
                    tgh = act(ar[:, AR_FUSEH + m * 8: AR_FUSEH + (m + 1) * 8],
                              P[hpar][:, 0:8], AF.Silu,
                              scale=cf[:, CF_SGATE + m:CF_SGATE + m + 1], wait=[tGh])
                    tfh = vop("scalar_tensor_tensor",
                              ar[:, AR_FUSEH + m * 8: AR_FUSEH + (m + 1) * 8],
                              P[hpar][:, 8:16], cf[:, CF_SUP + m:CF_SUP + m + 1],
                              ar[:, AR_FUSEH + m * 8: AR_FUSEH + (m + 1) * 8],
                              wait=[tUh, tgh], op0=ALU.mult, op1=ALU.mult)
                    setbf(hpar, tgh, tfh)
                    fuse_tok = tfh
            slot_use["gu"] = gu_used[MC - 1]
            dn_used = [None] * (2 * OC)
            x_tok = None
            xc = [None] * OC
            pend_stats = [None]
            for o in range(OC):
                par = o % 2
                htoks = []
                for half in range(2):
                    hs = (2 * o + half) % 4
                    wsem = "W%d" % hs
                    wwait = later(w_chain[wsem],
                                  slot_use["z"] if o < 2 else None,
                                  slot_use["wout"] if o < 2 else None,
                                  slot_use["qkv"] if o < 2 else None)
                    htoks.append(dma("gpsimd",
                                     ws[:, WS_DN + hs * 4096: WS_DN + (hs + 1) * 4096],
                                     wdn[o, :, half * 4096:(half + 1) * 4096],
                                     wsem, wait=wwait))
                tD = None
                for mi in range(MC):
                    half = mi // 32
                    hs = (2 * o + half) % 4
                    tD = mm(P[par],
                            ws[:, WS_DN + hs * 4096 + (mi % 32) * 128: WS_DN + hs * 4096 + (mi % 32 + 1) * 128],
                            ar[:, AR_FUSE + mi * T: AR_FUSE + (mi + 1) * T],
                            mi == 0, mi == MC - 1,
                            wait=later(htoks[half] if mi % 32 == 0 else None,
                                       fuse_tok if mi == 0 else None,
                                       *bf(par, mi == 0)),
                            inc=((("PE", 1),) if mi in (31, MC - 1) else ()))
                    if mi == 31:
                        dn_used[2 * o] = tD
                        w_chain["W%d" % ((2 * o) % 4)] = tD
                dn_used[2 * o + 1] = tD
                w_chain["W%d" % ((2 * o + 1) % 4)] = tD
                tDh = None
                if halo:
                    hpar = 2 if par == 0 else 5
                    for mi in range(MC):
                        half = mi // 32
                        hs = (2 * o + half) % 4
                        tDh = mm(P[hpar][:, 0:8],
                                 ws[:, WS_DN + hs * 4096 + (mi % 32) * 128: WS_DN + hs * 4096 + (mi % 32 + 1) * 128],
                                 ar[:, AR_FUSEH + mi * 8: AR_FUSEH + (mi + 1) * 8],
                                 mi == 0, mi == MC - 1,
                                 wait=later(*bf(hpar, mi == 0)),
                                 inc=((("PE", 1),) if mi in (31, MC - 1) else ()))
                        if mi == 31:
                            dn_used[2 * o] = tDh
                            w_chain["W%d" % ((2 * o) % 4)] = tDh
                    dn_used[2 * o + 1] = tDh
                    w_chain["W%d" % ((2 * o + 1) % 4)] = tDh
                tx = vop("scalar_tensor_tensor", xt_c(o), P[par],
                         cf[:, CF_SDH + o:CF_SDH + o + 1], xt_c(o),
                         wait=later(tD, tok_sdh), op0=ALU.mult, op1=ALU.add)
                setbf(par, tx)
                x_tok = tx
                xc[o] = tx
                if halo:
                    hpar = 2 if par == 0 else 5
                    txh = vop("scalar_tensor_tensor", xh_c(o), P[hpar][:, 0:8],
                              cf[:, CF_SDH + o:CF_SDH + o + 1], xh_c(o),
                              wait=[tDh], op0=ALU.mult, op1=ALU.add)
                    setbf(hpar, txh)
                    x_tok = txh
                    xc[o] = txh
                if next_stats is not None:
                    if pend_stats[0] is not None:
                        next_stats["cb"](*pend_stats[0])
                    pend_stats[0] = (o, xc[o])
            if next_stats is not None and pend_stats[0] is not None:
                next_stats["cb"](*pend_stats[0])
            slot_use["dn"] = dn_used[2 * OC - 1]
            arena_tok[0] = slot_use["dn"]
            return x_tok, xc

        def conv_col(base, h, j):
            return cf[:, base + h * 4 + j: base + h * 4 + j + 1]

        def emit_ATTN(ln_toks, ln_tok_h, p, next_stats=None):
            halo = (p == 0)
            ta = tb = tah = None
            for k in range(KC):
                ta = mm(P2[0:16, :], ws[:, WS_WAB + k * 32: WS_WAB + k * 32 + 16],
                        xln_c(k), k == 0, k == KC - 1,
                        wait=later(tok_wab, ln_toks[k], *bf(2, k == 0)),
                        inc=((("PE", 1),) if k == KC - 1 else ()))
            for k in range(KC):
                tb = mm(P5[0:16, :], ws[:, WS_WAB + k * 32 + 16: WS_WAB + k * 32 + 32],
                        xln_c(k), k == 0, k == KC - 1,
                        wait=later(*bf(5, k == 0)),
                        inc=((("PE", 1),) if k == KC - 1 else ()))
            if halo:
                for k in range(KC):
                    tah = mm(P6[0:16, 0:8], ws[:, WS_WAB + k * 32: WS_WAB + k * 32 + 16],
                             xlnh_c(k), k == 0, k == KC - 1,
                             wait=later(ln_tok_h if k == 0 else None, *bf(6, k == 0)),
                             inc=((("PE", 1),) if k == KC - 1 else ()))
            if p > 0:
                tc1 = vop("tensor_copy", rws(RW_A, 0, 8), rws(RW_A, 512, 520))
                tc2 = vop("tensor_copy", rws(RW_K2, 0, 8), rws(RW_K2, 512, 520),
                          wait=later(tile_free["k2row"]))
            else:
                tc1, tc2 = tok_z1, tok_z2
            tae = vop("tensor_copy", rws(RW_A, 8, TT), P2[0:16, :], wait=later(ta, tc1))
            if halo:
                tae = vop("tensor_copy", rws(RW_A, 0, 8), P6[0:16, 0:8], wait=later(tah, tae))
                setbf(6, tae)
            setbf(2, tae)
            gts = []
            for s in range(4):
                tg1 = vop("tensor_mul", rws(RW_S1, 8, 8 + T), rws(RW_A, 8 - s, TT - s),
                          P5[0:16, :],
                          wait=later(tae, tb, gts[-1] if s > 0 else None))
                gts.append(act(ar[0:16, AR_G4B + s * T: AR_G4B + (s + 1) * T],
                               rws(RW_S1, 8, 8 + T), AF.Sigmoid,
                               wait=later(tg1, arena_tok[0])))
            setbf(5, gts[-1])
            g4_tok = gts[-1]

            em.mark(f"attn{p}_loop1")
            qkv_used = [None] * H
            prev_ksw = None
            nk2h_evs = []
            dot_toks = [None] * 6
            convv_toks = [None] * H
            out_ready = [None] * H
            DMT2 = ((AR_DM, AR_DM1, AR_DM2, AR_DM3),
                    (AR_WBC0, AR_WBC1, AR_LT1, AR_SQ2))
            QQT = (AR_QQ, AR_QQB)
            KKT = (AR_KK, AR_KKB)
            pend = []
            pend_late = [None]

            def flush_dots(specs):
                for kind, hh, s, tdm in specs:
                    pp = hh % 2
                    ho1_ = cb[:, CB_HO1 + hh * 16: CB_HO1 + (hh + 1) * 16]
                    hos_ = cb[:, CB_HOS + hh * 16: CB_HOS + (hh + 1) * 16]
                    if kind == "q2":
                        dot_toks[0] = mm(P2[0:16, :], ho1_, arr(QQT[pp], 0, T),
                                         hh == 0, hh == H - 1,
                                         wait=later(tdm, *bf(2, hh == 0)),
                                         inc=(("PE", 1),))
                        tile_free["qq%d" % pp] = dot_toks[0]
                    elif kind == "k2":
                        dot_toks[1] = mm(P3[0:16, :], ho1_, arr(KKT[pp], 0, T),
                                         hh == 0, hh == H - 1,
                                         wait=later(tdm, *bf(3, hh == 0)),
                                         inc=(("PE", 1),))
                        tile_free["kk%d" % pp] = dot_toks[1]
                    else:
                        dbank = 4 + s
                        dot_toks[2 + s] = mm(P[dbank][0:16, :], hos_,
                                             arr(DMT2[pp][s], 0, T), hh == 0, hh == H - 1,
                                             wait=later(tdm, *bf(dbank, hh == 0)),
                                             inc=(("PE", 1),))
                        tile_free["dm%d%d" % (s, pp)] = dot_toks[2 + s]

            for h in range(H):
                par = h % 2
                slot = h % 3
                wsem = "W%d" % slot
                wwait = later(w_chain[wsem],
                              slot_use["gu"] if h < 3 else None,
                              slot_use["dn"] if h < 3 else None,
                              slot_use["z"] if h < 3 else None,
                              slot_use["wout"] if h < 3 else None)
                wtok = dma("gpsimd", ws[:, WS_QKV + slot * 6144: WS_QKV + (slot + 1) * 6144],
                           wqkv[h, :, :], wsem, wait=wwait)
                wtok_q = wtok_k = wtok

                def qkvslc(w_i, k, pp=slot):
                    b = WS_QKV + pp * 6144 + w_i * 2048 + k * 128
                    return ws[:, b:b + 128]
                A, B = (P0, P1)
                SQ_, SK_, SV_ = ((AR_SQ, AR_SK, AR_SV) if par == 0
                                 else (AR_SQB, AR_SKB, AR_SVB))
                tq = tk = tv = None
                for k in range(KC):
                    tq = mm(A, qkvslc(0, k), xln_c(k), k == 0, k == KC - 1,
                            wait=later(wtok_q, ln_toks[k], *bf(0, k == 0)),
                            inc=((("PE", 1),) if k == KC - 1 else ()))
                for k in range(KC):
                    tk = mm(B, qkvslc(1, k), xln_c(k), k == 0, k == KC - 1,
                            wait=later(wtok_k if k == 0 else None, *bf(1, k == 0)),
                            inc=((("PE", 1),) if k == KC - 1 else ()))
                evq = act(arr(SQ_, 8, TT), A, AF.Silu,
                          wait=later(tq, arena_tok[0], tile_free["sq%d" % par]))
                for k in range(KC):
                    tv = mm(A, qkvslc(2, k), xln_c(k), k == 0, k == KC - 1,
                            wait=later(evq if k == 0 else None, wtok if k == 0 else None),
                            inc=((("PE", 1),) if k == KC - 1 else ()))
                if len(pend) == 2:
                    flush_dots(pend.pop(0))
                evk = act(arr(SK_, 8, TT), B, AF.Silu,
                          wait=later(tk, tile_free["sk%d" % par]))
                th = None
                if halo:
                    for w_i in range(3):
                        for k in range(KC):
                            th = mm(B[:, w_i * 8:(w_i + 1) * 8], qkvslc(w_i, k), xlnh_c(k),
                                    k == 0, k == KC - 1,
                                    wait=later((evk if (w_i == 0 and k == 0) else None),
                                               (ln_tok_h if (w_i == 0 and k == 0) else None)),
                                    inc=((("PE", 1),) if k == KC - 1 else ()))
                evv = act(arr(SV_, 8, TT), A, AF.Silu,
                          wait=later(tv, tile_free["sv%d" % par]))
                setbf(0, evv)
                if halo:
                    e1 = act(arr(SQ_, 0, 8), B[:, 0:8], AF.Silu, wait=[th])
                    e2 = act(arr(SK_, 0, 8), B[:, 8:16], AF.Silu, wait=[th])
                    e3 = act(arr(SV_, 0, 8), B[:, 16:24], AF.Silu, wait=[th])
                    fq, fk, fv = e1, e2, e3
                else:
                    fk = act(arr(SK_, 0, 8), carSk[:, h * 8:(h + 1) * 8], AF.Copy)
                    fq = act(arr(SQ_, 0, 8), carSq[:, h * 8:(h + 1) * 8], AF.Copy)
                    fv = act(arr(SV_, 0, 8), carSv[:, h * 8:(h + 1) * 8], AF.Copy)
                if pend_late[0] is not None:
                    pend_late[0]()
                    pend_late[0] = None
                def conv(dst_base, dst_lo, dst_hi, src_base, wbase, waits):
                    t0 = vop("tensor_scalar", arr(dst_base, dst_lo, dst_hi),
                             arr(src_base, dst_lo - 3, dst_hi - 3),
                             conv_col(wbase, h, 0), None,
                             wait=waits, op0=ALU.mult)
                    tl = t0
                    for j in (1, 2, 3):
                        tl = vop("scalar_tensor_tensor", arr(dst_base, dst_lo, dst_hi),
                                 arr(src_base, dst_lo - (3 - j), dst_hi - (3 - j)),
                                 conv_col(wbase, h, j), arr(dst_base, dst_lo, dst_hi),
                                 wait=[tl], op0=ALU.mult, op1=ALU.add)
                    return tl
                tck = conv(AR_CK, 8, TT, SK_, CF_CONVK,
                           later(evk, fk, prev_ksw, tile_free["late_kk"]))
                if halo:
                    tckh = conv(AR_CK, 3, 8, SK_, CF_CONVK, later(fk, prev_ksw))
                    vop("tensor_copy", arr(AR_CKH, 5 * h, 5 * h + 5), arr(AR_CK, 3, 8),
                        wait=later(tckh))
                else:
                    tckh = act(arr(AR_CK, 0, 8), carCk[:, h * 8:(h + 1) * 8], AF.Copy,
                               wait=later(tile_free["ckrd"]))
                ks1 = em.emit("sync", lambda e: e.dma_start(
                    out=ar[0:64, AR_KSW:AR_KSW + TT], in_=ar[64:128, AR_CK:AR_CK + TT]),
                    wait=(("DV", em.cnt["DV"]), ("AC", em.cnt["AC"])), inc=(("SW", 16),))
                ks2 = em.emit("sync", lambda e: e.dma_start(
                    out=ar[64:128, AR_KSW:AR_KSW + TT], in_=ar[0:64, AR_CK:AR_CK + TT]),
                    inc=(("SW", 16),))
                kswap_tok = ks2
                prev_ksw = ks2
                tcq = conv(AR_CQ, 8, TT, SQ_, CF_CONVQ, later(evq, fq))
                tcv = conv(AR_CV + h * TT, 8, TT, SV_, CF_CONVV, later(evv, fv))
                if halo:
                    tcvh = conv(AR_CV + h * TT, 3, 8, SV_, CF_CONVV, [fv])
                else:
                    tcvh = act(arr(AR_CV + h * TT, 0, 8), carCv[:, h * 8:(h + 1) * 8],
                               AF.Copy, wait=later(convv_free[h]))
                convv_toks[h] = later(tcv, tcvh)
                tile_free["sq%d" % par] = [tcq, act(carSq[:, h * 8:(h + 1) * 8],
                                                    arr(SQ_, 512, 520), AF.Copy)]
                tile_free["sk%d" % par] = [tck, act(carSk[:, h * 8:(h + 1) * 8],
                                                    arr(SK_, 512, 520), AF.Copy)]
                tile_free["sv%d" % par] = [tcv, act(carSv[:, h * 8:(h + 1) * 8],
                                                    arr(SV_, 512, 520), AF.Copy)]
                specs = []
                qq_ref, kk_ref = [None], [None]

                def late_ops(h=h, par=par, tck=tck, tcq=tcq, tcv=tcv,
                             qq_ref=qq_ref, kk_ref=kk_ref):
                    act(carCk[:, h * 8:(h + 1) * 8], arr(AR_CK, 512, 520), AF.Copy,
                        wait=[tck])
                    act(carCv[:, h * 8:(h + 1) * 8], arr(AR_CV + h * TT, 512, 520),
                        AF.Copy, wait=[tcv])
                    qq_ref[0] = act(arr(QQT[par], 0, T), arr(AR_CQ, 8, TT), AF.Square,
                                    wait=later(tcq, tile_free["qq%d" % par]))
                    kk_ref[0] = act(arr(KKT[par], 0, T), arr(AR_CK, 8, TT), AF.Square,
                                    wait=later(tck, tile_free["kk%d" % par]))
                    tile_free["late_kk"] = kk_ref[0]
                pend_late[0] = late_ops
                specs.append(("q2", h, 0, qq_ref))
                specs.append(("k2", h, 0, kk_ref))
                ho1 = cb[:, CB_HO1 + h * 16: CB_HO1 + (h + 1) * 16]
                for s in range(4):
                    if s == 0:
                        tdm = vop("tensor_mul", arr(DMT2[par][0], 0, T), arr(AR_CQ, 8, TT),
                                  arr(AR_CK, 8, TT),
                                  wait=later(tcq, tck, tckh, tile_free["dm0%d" % par]))
                    else:
                        tr1 = vop("tensor_scalar", arr(AR_LT0, 0, TT, (0, 128)),
                                  arr(AR_KSW, 0, TT),
                                  cf[:, CF_SINP + s - 1:CF_SINP + s], None,
                                  wait=later(kswap_tok), op0=ALU.mult)
                        tr2 = vop("scalar_tensor_tensor", arr(AR_RK, 0, TT),
                                  arr(AR_CK, 0, TT), cf[:, CF_COS + s - 1:CF_COS + s],
                                  arr(AR_LT0, 0, TT, (0, 128)),
                                  wait=later(tr1, tckh if s == 1 else None),
                                  op0=ALU.mult, op1=ALU.add)
                        tile_free["ckrd"] = tr2
                        tdm = vop("tensor_mul", arr(DMT2[par][s], 0, T), arr(AR_CQ, 8, TT),
                                  arr(AR_RK, 8 - s, TT - s),
                                  wait=later(tr2, tile_free["dm%d%d" % (s, par)]))
                    specs.append(("d", h, s, tdm))
                pend.append(specs)
                if halo:
                    setbf(1, e3)
                else:
                    setbf(1, evk)
                qkv_used[h] = th if halo else tv
                w_chain[wsem] = qkv_used[h]
            if pend_late[0] is not None:
                pend_late[0]()
                pend_late[0] = None
            while pend:
                flush_dots(pend.pop(0))
            slot_use["qkv"] = qkv_used[H - 1]
            nk2h_tok = None
            if halo:
                tsqh = vop("tensor_mul", arr(AR_CKH, 0, 80), arr(AR_CKH, 0, 80),
                           arr(AR_CKH, 0, 80), wait=later(*nk2h_evs))
                tnk = None
                for h2 in range(H):
                    tnk = mm(P0[0:16, 0:5], cb[:, CB_HO1 + h2 * 16: CB_HO1 + (h2 + 1) * 16],
                             arr(AR_CKH, 5 * h2, 5 * h2 + 5), h2 == 0, h2 == H - 1,
                             wait=later(tsqh, *bf(0, h2 == 0)),
                             inc=((("PE", 1),) if h2 == H - 1 else ()))
                nk2h_tok = vop("tensor_add", rows[0:16, RW_K2 + 3:RW_K2 + 8],
                               rows[0:16, RW_K2 + 3:RW_K2 + 8],
                               P0[0:16, 0:5], wait=[tnk, tok_z2])
                setbf(0, nk2h_tok)
                nk2h_evs = [nk2h_tok]

            em.mark(f"attn{p}_rows")
            tk2 = vop("tensor_copy", rws(RW_K2, 8, TT), P3[0:16, :],
                      wait=later(dot_toks[1], tc2, *nk2h_evs))
            setbf(3, tk2)
            t1 = act(rws(RW_S1, 0, T), P2[0:16, :], AF.Sqrt,
                     wait=later(dot_toks[0], gts[-1]))
            setbf(2, t1)
            t2 = vop("tensor_scalar_max", rws(RW_S1, 0, T), rws(RW_S1, 0, T), 1e-12, wait=[t1])
            trq = vop("reciprocal", rws(RW_RQ, 0, T), rws(RW_S1, 0, T), wait=[t2])
            t3 = act(rws(RW_S3, 0, TT), rws(RW_K2, 0, TT), AF.Sqrt, wait=[tk2])
            tile_free["k2row"] = t3
            t4 = vop("tensor_scalar_max", rws(RW_S3, 0, TT), rws(RW_S3, 0, TT), 1e-12, wait=[t3])
            trk = vop("reciprocal", rws(RW_RK, 0, TT), rws(RW_S3, 0, TT), wait=[t4])
            w4_toks = []
            for s in range(4):
                tw = vop("tensor_mul", rws(RW_S1, 0, T), P[4 + s][0:16, :],
                         ar[0:16, AR_G4B + s * T: AR_G4B + (s + 1) * T],
                         wait=later(dot_toks[2 + s], trq))
                setbf(4 + s, tw)
                tw = vop("tensor_mul", rws(RW_S1, 0, T), rws(RW_S1, 0, T), rws(RW_RQ, 0, T),
                         wait=[tw])
                tw = vop("tensor_mul", rws(RW_S1, 0, T), rws(RW_S1, 0, T),
                         rws(RW_RK, 8 - s, TT - s), wait=later(tw, trk))
                w4_toks.append(vop("tensor_copy",
                                   ar[0:16, AR_W4B + s * T: AR_W4B + (s + 1) * T],
                                   rws(RW_S1, 0, T), wait=[tw]))

            em.mark(f"attn{p}_loop2")
            z_used = [None] * H
            tsz_toks = [None] * H
            rms_tok = None
            pend_rms = [None]
            for h in range(H):
                par = h % 2
                wsem = "W0" if par == 0 else "W1"
                wwait = later(w_chain[wsem])
                wtok = dma("gpsimd", ws[:, WS_Z + par * 2048: WS_Z + (par + 1) * 2048],
                           wz[h, :, :], wsem, wait=wwait)
                zbank = 2 if par == 0 else 5
                tz = None
                for k in range(KC):
                    tz = mm(P[zbank], ws[:, WS_Z + par * 2048 + k * 128: WS_Z + par * 2048 + (k + 1) * 128],
                            xln_c(k), k == 0, k == KC - 1,
                            wait=later(wtok, *bf(zbank, k == 0)),
                            inc=((("PE", 1),) if k == KC - 1 else ()))
                z_used[h] = tz
                w_chain[wsem] = tz
                tsz = act(siluz[:, h * T:(h + 1) * T], P[zbank], AF.Silu, wait=[tz])
                tsz_toks[h] = tsz
                setbf(zbank, tsz)
                eh = cb[0:16, CB_EH + h * 128: CB_EH + (h + 1) * 128]
                mts = []
                for s in range(4):
                    wb = (0, 1, 6, 7)[s]
                    tb_ = mm(P[wb], eh, ar[0:16, AR_W4B + s * T: AR_W4B + (s + 1) * T],
                             True, True,
                             wait=later(w4_toks[s], *bf(wb)), inc=(("PE", 1),))
                    dst = (AR_DM, AR_DM1, AR_DM2, AR_DM3)[s]
                    mts.append(vop("tensor_mul", arr(dst, 0, T),
                                   P[wb],
                                   arr(AR_CV + h * TT, 8 - s, TT - s),
                                   wait=later(tb_, convv_toks[h],
                                              tile_free["dm%d0" % s])))
                    setbf(wb, mts[-1])
                    tile_free["dm%d0" % s] = None
                convv_free[h] = mts[3]
                s01 = vop("tensor_add", arr(AR_DM, 0, T), arr(AR_DM, 0, T), arr(AR_DM1, 0, T),
                          wait=later(mts[0], mts[1]))
                s23 = vop("tensor_add", arr(AR_DM2, 0, T), arr(AR_DM2, 0, T), arr(AR_DM3, 0, T),
                          wait=later(mts[2], mts[3]))
                tout = vop("tensor_add", arr(AR_OUT + h * T, 0, T), arr(AR_DM, 0, T),
                           arr(AR_DM2, 0, T), wait=later(s01, s23))
                out_ready[h] = tout
                if pend_rms[0] is not None:
                    hh, tt = pend_rms[0]
                    tsq_ = act(arr(AR_QQ, 0, T), arr(AR_OUT + hh * T, 0, T), AF.Square,
                               wait=later(tt, tile_free["qq"]))
                    rms_tok = mm(P3[0:16, :], cb[:, CB_HO1 + hh * 16: CB_HO1 + (hh + 1) * 16],
                                 arr(AR_QQ, 0, T), hh == 0, hh == H - 1,
                                 wait=later(tsq_, *bf(3, hh == 0)),
                                 inc=(("PE", 1),))
                    tile_free["qq"] = rms_tok
                pend_rms[0] = (h, tout)
            hh, tt = pend_rms[0]
            tsq_ = act(arr(AR_QQ, 0, T), arr(AR_OUT + hh * T, 0, T), AF.Square,
                       wait=later(tt, tile_free["qq"]))
            rms_tok = mm(P3[0:16, :], cb[:, CB_HO1 + hh * 16: CB_HO1 + (hh + 1) * 16],
                         arr(AR_QQ, 0, T), hh == 0, hh == H - 1,
                         wait=later(tsq_, *bf(3, hh == 0)),
                         inc=(("PE", 1),))
            tile_free["qq"] = rms_tok
            pend_rms[0] = None
            slot_use["z"] = z_used[H - 1]

            em.mark(f"attn{p}_rmsf")
            trm = vop("tensor_copy", rws(RW_S1, 0, T), P3[0:16, :], wait=[rms_tok])
            setbf(3, trm)
            tsum = mm(P4[0:1, :], cf[0:16, CF_ONE16:CF_ONE16 + 1], rws(RW_S1, 0, T),
                      True, True, wait=later(trm, *bf(4)), inc=(("PE", 1),))
            tr1_ = vop("tensor_copy", rows[0:1, RW_S3:RW_S3 + T], P4[0:1, :], wait=[tsum])
            setbf(4, tr1_)
            tr2_ = act(rows[0:1, RW_S1:RW_S1 + T], rows[0:1, RW_S3:RW_S3 + T], AF.Sqrt,
                       scale=cf[0:1, CF_INVD:CF_INVD + 1],
                       bias=cf[0:1, CF_RMSEPS:CF_RMSEPS + 1], wait=[tr1_])
            tr3_ = vop("reciprocal", rows[0:1, RW_S3:RW_S3 + T], rows[0:1, RW_S1:RW_S1 + T],
                       wait=[tr2_])
            tbc = mm(P4, cf[0:1, CF_ONEBC:CF_ONEBC + 128], rows[0:1, RW_S3:RW_S3 + T],
                     True, True, wait=later(tr3_, *bf(4)), inc=(("PE", 1),))

            em.mark(f"attn{p}_l3a")
            l3 = []
            for h in range(H):
                t1_ = vop("scalar_tensor_tensor", arr(AR_QQ, 0, T), arr(AR_OUT + h * T, 0, T),
                          cf[:, CF_NORMW + h:CF_NORMW + h + 1], P4,
                          wait=later(tbc, out_ready[h],
                                     tile_free["qq"] if h == 0 else None),
                          op0=ALU.mult, op1=ALU.mult)
                l3.append(vop("tensor_mul", siluz[:, h * T:(h + 1) * T],
                              siluz[:, h * T:(h + 1) * T], arr(AR_QQ, 0, T),
                              wait=later(t1_, tsz_toks[h])))
            setbf(4, l3[-1])
            win_tok = later(l3[-1])

            em.mark(f"attn{p}_l3b")
            wo_used = [None] * OC
            x_tok = None
            xc = [None] * OC
            pend_stats = [None]
            for o in range(OC):
                par = o % 2
                wsem = "W0" if par == 0 else "W1"
                wwait = later(w_chain[wsem],
                              slot_use["qkv"] if o < 2 else None)
                wtok = dma("gpsimd", ws[:, WS_WOUT + par * 2048: WS_WOUT + (par + 1) * 2048],
                           wou[o, :, :], wsem, wait=wwait)
                tW = None
                for k in range(KC):
                    tW = mm(P[par], ws[:, WS_WOUT + par * 2048 + k * 128: WS_WOUT + par * 2048 + (k + 1) * 128],
                            siluz[:, k * T:(k + 1) * T], k == 0, k == KC - 1,
                            wait=later(wtok, win_tok, *bf(par, k == 0)),
                            inc=((("PE", 1),) if k == KC - 1 else ()))
                wo_used[o] = tW
                w_chain[wsem] = tW
                tx = vop("tensor_add", xt_c(o), xt_c(o), P[par], wait=[tW])
                setbf(par, tx)
                x_tok = tx
                xc[o] = tx
                if next_stats is not None:
                    if pend_stats[0] is not None:
                        next_stats["cb"](*pend_stats[0])
                    pend_stats[0] = (o, tx)
            if next_stats is not None and pend_stats[0] is not None:
                next_stats["cb"](*pend_stats[0])
            slot_use["wout"] = wo_used[OC - 1]
            arena_tok[0] = x_tok
            return x_tok, xc

        store_cnt = 0
        st1_next = [None]

        def mk_post2(p, st1n):
            load_tok = [None] * KC

            def cbk(o, tx):
                dma("sync", yTr[o, :, p * T:(p + 1) * T], xt_c(o), "ST",
                    wait=[tx])
                if p + 1 < NPASS:
                    load_tok[o] = dma("sync", xt_c(o),
                                      xTr[o, :, (p + 1) * T:(p + 2) * T], "LX",
                                      wait=[("ST", em.cnt["ST"]),
                                            ("LX", em.cnt["LX"])])
                    if o >= 3 and st1n is not None:
                        st1n["cb"](o - 3, load_tok[o - 3])
            return cbk, load_tok

        for p in range(NPASS):
            halo = (p == 0)
            if halo:
                xl = []
                for c in range(KC):
                    xl.append(dma("sync", xt_c(c), xTr[c, :, p * T:(p + 1) * T],
                                  "LX"))
                xt0 = later(xl[-1], tok_xh, tok_cf, tok_cb)
                em.mark(f"p{p}_LN1")
                ln1, ln1h = emit_LN(lambda c: xt_c(c), lambda c: xh_c(c),
                                    lambda c: xln_c(c), lambda c: xlnh_c(c),
                                    lambda c: xt0, halo, f"ln1p{p}")
            else:
                em.mark(f"p{p}_LN1")
                ln1, ln1h = emit_LN(lambda c: xt_c(c), lambda c: xh_c(c),
                                    lambda c: xln_c(c), lambda c: xlnh_c(c),
                                    lambda c: None, False, f"ln1p{p}",
                                    stats=st1_next[0])
            em.mark(f"p{p}_FFN1")
            st2 = None if halo else ln_stats(lambda c: xt_c(c), single=True)
            xf, xf_c = emit_FFN(ln1, ln1h, halo, next_stats=st2)
            em.mark(f"p{p}_LN2")
            ln2, ln2h = emit_LN(lambda c: xt_c(c), lambda c: xh_c(c),
                                lambda c: xln_c(c), lambda c: xlnh_c(c),
                                lambda c: xf_c[c], halo, f"ln2p{p}", stats=st2)
            em.mark(f"p{p}_ATTN")
            st3 = ln_stats(lambda c: xt_c(c), single=True)
            xa, xa_c = emit_ATTN(ln2, ln2h, p, next_stats=st3)
            em.mark(f"p{p}_LN3")
            ln3, _ = emit_LN(lambda c: xt_c(c), lambda c: xh_c(c),
                             lambda c: xln_c(c), lambda c: xlnh_c(c),
                             lambda c: xa_c[c], False, f"ln3p{p}", stats=st3)
            em.mark(f"p{p}_FFN2")
            st1n = ln_stats(lambda c: xt_c(c), single=True) if p + 1 < NPASS else None
            post2, load_tok = mk_post2(p, st1n)
            xf2, xf2_c = emit_FFN(ln3, None, False, next_stats={"cb": post2})
            if st1n is not None:
                for c in range(KC - 3, KC):
                    st1n["cb"](c, load_tok[c])
            st1_next[0] = st1n

        em.emit("sync", lambda e: e.nop(), wait=[("ST", em.cnt["ST"])])

        def run_ops(eng_obj, name):
            last = {}
            rec = build_program._inst_names.setdefault(name, [])
            for fn, waits, incs in em.ops[name]:
                for s, v in waits:
                    if last.get(s, 0) < v:
                        eng_obj.wait_ge(sems[s], v)
                        last[s] = v
                inst = fn(eng_obj)
                rec.append(getattr(inst.ins, "name", None))
                for s, n in incs:
                    inst.then_inc(sems[s], n)

        import json as _json
        build_program._inst_names = {}
        build_program._marks = em.marks
        build_program._n_tensor = len(em.ops["tensor"])

        @block.sync
        def _(e):
            run_ops(e, "sync")

        @block.gpsimd
        def _(e):
            run_ops(e, "gpsimd")

        @block.tensor
        def _(e):
            run_ops(e, "tensor")

        @block.vector
        def _(e):
            run_ops(e, "vector")

        @block.scalar
        def _(e):
            run_ops(e, "scalar")

    return nc


def _host_prep(inputs):
    f32 = np.float32
    Wq = np.asarray(inputs["L0_attn_Wq"], f32)
    Wk = np.asarray(inputs["L0_attn_Wk"], f32)
    Wv = np.asarray(inputs["L0_attn_Wv"], f32)
    Wz = np.asarray(inputs["L0_attn_Wz"], f32)
    Wa = np.asarray(inputs["L0_attn_Wa"], f32)
    Wb = np.asarray(inputs["L0_attn_Wb"], f32)
    Wout = np.asarray(inputs["L0_attn_Wout"], f32)
    w_gate = np.asarray(inputs["L0_ffn_w_gate"], np.int8)
    w_up = np.asarray(inputs["L0_ffn_w_up"], np.int8)
    w_down = np.asarray(inputs["L0_ffn_w_down"], np.int8)

    arr = np.stack([w_gate, w_up], axis=0).reshape(2, KC, 128, MC, 128)
    wgu = np.ascontiguousarray(arr.transpose(3, 2, 0, 1, 4).reshape(MC, 128, 2 * KC * 128))
    wdn = np.ascontiguousarray(
        w_down.reshape(MC, 128, OC, 128).transpose(2, 1, 0, 3).reshape(OC, 128, MC * 128))
    a3 = np.stack([Wq, Wk, Wv], axis=0).reshape(3, KC, 128, OC, 128)
    wqkv = np.ascontiguousarray(a3.transpose(3, 2, 0, 1, 4).reshape(OC, 128, 3 * KC * 128))
    wz = np.ascontiguousarray(
        Wz.reshape(KC, 128, OC, 128).transpose(2, 1, 0, 3).reshape(OC, 128, KC * 128))
    wou = np.ascontiguousarray(
        Wout.reshape(KC, 128, OC, 128).transpose(2, 1, 0, 3).reshape(OC, 128, KC * 128))
    wab = np.ascontiguousarray(
        np.concatenate([Wa, Wb], axis=1).reshape(KC, 128, 32).transpose(1, 0, 2).reshape(128, KC * 32))

    cf = np.zeros((128, NCF), f32)
    cf[:, CF_SGATE:CF_SGATE + 64] = np.asarray(inputs["L0_ffn_s_gate"], f32).reshape(64, 128).T
    cf[:, CF_SUP:CF_SUP + 64] = np.asarray(inputs["L0_ffn_s_up"], f32).reshape(64, 128).T
    cf[:, CF_SDOWN:CF_SDOWN + 16] = np.asarray(inputs["L0_ffn_s_down"], f32).reshape(16, 128).T
    cf[:, CF_NORMW:CF_NORMW + 16] = np.asarray(inputs["L0_attn_norm_w"], f32).reshape(16, 128).T
    for base, name in ((CF_CONVQ, "L0_attn_conv_q_w"), (CF_CONVK, "L0_attn_conv_k_w"),
                       (CF_CONVV, "L0_attn_conv_v_w")):
        cw = np.asarray(inputs[name], f32).reshape(16, 128, 4).transpose(1, 0, 2).reshape(128, 64)
        cf[:, base:base + 64] = cw
    theta = (1.0 / 1e6 ** (np.arange(0, 128, 2, dtype=np.float64) / 128.0))
    th128 = np.concatenate([theta, theta])
    for s in (1, 2, 3):
        cf[:, CF_COS + s - 1] = np.cos(s * th128).astype(f32)
        sp = np.sin(s * th128).astype(f32)
        sp[64:] = -sp[64:]
        cf[:, CF_SINP + s - 1] = sp
    cf[:, CF_ONE] = 1.0
    cf[0, CF_MEANBC:CF_MEANBC + 128] = 1.0 / HID
    cf[0, CF_ONEBC:CF_ONEBC + 128] = 1.0
    cf[:, CF_ONE16] = 1.0
    cf[:, CF_LNEPS] = 1e-5
    cf[:, CF_RMSEPS] = 1e-6
    cf[:, CF_INVD] = 1.0 / HID

    cb = np.zeros((128, NCB), ml_dtypes.bfloat16)
    ho = np.zeros((128, 256), np.float32)
    for h in range(H):
        ho[:, h * 16 + h] = 1.0
    cb[:, CB_HO1:CB_HO1 + 256] = ho.astype(ml_dtypes.bfloat16)
    cb[:, CB_HOS:CB_HOS + 256] = (ho / np.sqrt(128.0)).astype(ml_dtypes.bfloat16)
    cb[:, CB_ONECOL] = ml_dtypes.bfloat16(1.0)
    eh = np.zeros((128, 2048), np.float32)
    for h in range(H):
        eh[h, h * 128:(h + 1) * 128] = 1.0
    cb[:, CB_EH:CB_EH + 2048] = eh.astype(ml_dtypes.bfloat16)

    shared = {"wgu": wgu, "wdn": wdn, "wqkv": wqkv, "wz": wz, "wou": wou,
              "wab": wab, "cfd": cf, "cbd": cb}
    return shared


def kernel(**inputs):
    x = np.asarray(inputs["L0_attn_x"], np.float32)
    Bsz, S, _ = x.shape
    shared = _host_prep(inputs)
    in_maps = []
    for c in range(NCORE):
        b, half = c // 2, c % 2
        t0 = half * 2048
        sl = x[b, t0:t0 + 2048, :]
        if half == 0:
            hl = np.zeros((HALO, HID), np.float32)
        else:
            hl = x[b, t0 - HALO:t0, :]
        xT = np.ascontiguousarray(sl.T)
        hT = hl.T
        xh = np.ascontiguousarray(hT.reshape(KC, 128, HALO).transpose(1, 0, 2).reshape(128, KC * HALO))
        m = dict(shared)
        m["xT"] = xT
        m["xH"] = xh
        in_maps.append(m)
    nc = build_program()
    res = run_bass_kernel_spmd(nc, in_maps, core_ids=list(range(NCORE)))
    out = np.empty((Bsz, S, HID), np.float32)
    for c in range(NCORE):
        b, half = c // 2, c % 2
        out[b, half * 2048:(half + 1) * 2048, :] = res.results[c]["yT"].T
    return out
